# version 12
# baseline (speedup 1.0000x reference)
import numpy as np
import concourse.bass as bass
import concourse.mybir as mybir
from concourse.bass_utils import run_bass_kernel_spmd

F32 = mybir.dt.float32
BF16 = mybir.dt.bfloat16
I32 = mybir.dt.int32
Alu = mybir.AluOpType
Act = mybir.ActivationFunctionType
AxX = mybir.AxisListType.X

S = 2048
D = 2048
C = 16
NT = 16
TG = 4
NH = 16
GK = 4
KD = 512

R1_OFF = 32704
R2_OFF = 98240
R3_OFF = 163776
KB = 1024

INV2048 = float(np.float32(1.0 / 2048.0))
INV127 = float(np.float32(1.0 / 127.0))
MAGIC = 12582912.0


class Emit:
    ENGS = ("sync", "tensor", "vector", "scalar", "gpsimd")

    def __init__(self):
        self.scripts = {e: [] for e in self.ENGS}
        self.ctr = {}
        self.waited = {e: {} for e in self.ENGS}
        self.sems = {}

    def inc_val(self, sem, n):
        self.ctr[sem] = self.ctr.get(sem, 0) + n
        return self.ctr[sem]

    def cur(self, sem):
        return self.ctr.get(sem, 0)

    def wait(self, eng, sem, val):
        if val is None or val <= 0:
            return
        if self.waited[eng].get(sem, 0) >= val:
            return
        self.waited[eng][sem] = val
        self.scripts[eng].append(
            lambda e, s=sem, v=val: e.wait_ge(self.sems[s], v))

    def waits(self, eng, ws):
        for w in ws:
            if w is None:
                continue
            if isinstance(w, list):
                self.waits(eng, w)
            else:
                self.wait(eng, w[0], w[1])

    def op(self, eng, fn, inc=None, waits=()):
        self.waits(eng, waits)
        if inc is None:
            self.scripts[eng].append(fn)
            return None
        n = self.inc_val(inc, 1)
        self.scripts[eng].append(
            lambda e, f=fn, s=inc: f(e).then_inc(self.sems[s], 1))
        return n

    def vop(self, fn, waits=()):
        n = self.op("vector", fn, inc="vd", waits=waits)
        self.wait("vector", "vd", n)
        return n

    def dma(self, eng, out, in_, sem, waits=(), **kw):
        self.waits(eng, waits)
        n = self.inc_val(sem, 16)
        self.scripts[eng].append(
            lambda e, o=out, i=in_, s=sem, kw=kw: e.dma_start(out=o, in_=i, **kw)
            .then_inc(self.sems[s], 16))
        return n


class Banks:
    def __init__(self, em, pb):
        self.em = em
        self.pb = pb
        self.last = [None] * len(pb)
        self.rr = 0
        self.pinned = set()

    def take(self, n=1):
        nb = len(self.pb)
        for _ in range(nb):
            s = self.rr % nb
            self.rr += 1
            if s + n > nb:
                continue
            if any((s + k) in self.pinned for k in range(n)):
                continue
            for k in range(n):
                if self.last[s + k] is not None:
                    self.em.wait("tensor", self.last[s + k][0], self.last[s + k][1])
            self.rr = s + n
            return s
        raise RuntimeError("no free psum bank group")

    def done(self, i, sem, cnt):
        self.last[i] = (sem, cnt)

    def pin(self, i):
        self.pinned.add(i)
        if self.last[i] is not None:
            self.em.wait("tensor", self.last[i][0], self.last[i][1])

    def unpin(self, i, sem, cnt):
        self.pinned.discard(i)
        self.last[i] = (sem, cnt)


def build_nc():
    nc = bass.Bass("TRN2", target_bir_lowering=False,
                   detect_race_conditions=False)
    em = Emit()

    dx = nc.dram_tensor("x_nat", [S, D], F32, kind="ExternalInput").ap()
    dxp = nc.dram_tensor("x_pk", [C * TG * 128, 512], F32, kind="ExternalInput").ap()
    dwq = nc.dram_tensor("wq_pk", [NH * 128, D], F32, kind="ExternalInput").ap()
    dwo = nc.dram_tensor("wo_pk", [C * 128, D], F32, kind="ExternalInput").ap()
    dwk = nc.dram_tensor("wk_pk", [128, C * KD], F32, kind="ExternalInput").ap()
    dwv = nc.dram_tensor("wv_pk", [128, C * KD], F32, kind="ExternalInput").ap()
    dwn = nc.dram_tensor("wn_nat", [D, D], F32, kind="ExternalInput").ap()
    dbq = nc.dram_tensor("bq_t", [128, 16], F32, kind="ExternalInput").ap()
    dfb = nc.dram_tensor("fb_t", [128, 16], F32, kind="ExternalInput").ap()
    dbk = nc.dram_tensor("bk_r", [1, KD], F32, kind="ExternalInput").ap()
    dbv = nc.dram_tensor("bv_r", [1, KD], F32, kind="ExternalInput").ap()
    dsel = nc.dram_tensor("sel_h", [16, D], F32, kind="ExternalInput").ap()
    dident = nc.dram_tensor("ident", [128, 128], F32, kind="ExternalInput").ap()
    dout = nc.dram_tensor("out", [D, S], F32, kind="ExternalOutput").ap()
    d_wq2 = nc.dram_tensor("wq_sc", [C * C * 128, 128], BF16, kind="Internal").ap()
    d_xqsc = nc.dram_tensor("xq_sc", [S, D], BF16, kind="Internal").ap()
    d_xnsc = nc.dram_tensor("xn_sc", [S, D], BF16, kind="Internal").ap()
    d_xsr = nc.dram_tensor("xsr_sc", [1, S], F32, kind="Internal").ap()

    A = nc.alloc_sbuf_tensor
    ident = A("sb_ident", [128, 128], F32)
    ident_bf = A("sb_identb", [128, 128], BF16)
    ones_rb = A("ones_rb", [1, 128], BF16)
    ones_cb = A("ones_cb", [128, 1], BF16)
    ones_rf = A("ones_rf", [1, 128], F32)
    stn = ["sum", "mean", "mbias", "ssq", "ve", "r", "amax", "xs", "rxs",
           "s3", "t1", "t2", "wmax", "wmaxb", "wsc", "rws"]
    st = {k: A("st_" + k, [128, 16], F32) for k in stn}
    bq_t = A("sb_bq", [128, 16], F32)
    bq1_t = A("bq1_t", [128, 16], F32)
    fb_t = A("sb_fb", [128, 16], F32)
    bk_r = A("sb_bk", [1, KD], BF16)
    bv_r = A("sb_bv", [1, KD], BF16)
    kv_sb = A("kv_sb", [128, 4 * 132], BF16)
    kemb = A("kemb", [128, 256], BF16)
    e_bf = [A(f"e_bf{i}", [128, 512], BF16) for i in range(2)]
    rp_bf = [A(f"rp_bf{i}", [128, 512], BF16) for i in range(2)]
    cns0 = A("cns0", [128, 1], F32)
    cns1 = A("cns1", [128, 1], F32)
    mgc = A("mgc", [128, 1], F32)
    bqc = [A(f"bqc{i}", [128, 1], F32) for i in range(16)]
    fbc = [A(f"fbc{i}", [128, 1], F32) for i in range(16)]
    wscc = [A(f"wscc{i}", [128, 1], F32) for i in range(16)]
    mbc = [A(f"mbc{i}", [128, 1], F32) for i in range(2)]
    rc = [A(f"rc{i}", [128, 1], F32) for i in range(2)]
    rwc = [A(f"rwc{i}", [128, 1], F32) for i in range(2)]
    wqT_st = [A(f"wqT{i}", [128, 128], BF16) for i in range(4)]
    rec_bf = A("rec_bf", [16, 512], BF16)
    assert nc.sbuf_base <= R1_OFF, f"persist overflow: {nc.sbuf_base}"

    AT = nc.alloc_sbuf_tensor_at
    xn_fm = AT("xn_fm", [128, C * S], BF16, offset=R1_OFF)
    attn_fm = AT("attn_fm", [128, C * S], BF16, offset=R1_OFF)
    wkT = AT("wkT", [128, C * KD], BF16, offset=R2_OFF)
    wvT = AT("wvT", [128, C * KD], BF16, offset=R2_OFF + 16 * KB)
    sk_tm = AT("sk_tm", [128, NT * KD], BF16, offset=R2_OFF + 32 * KB)
    v_tm = AT("v_tm", [128, NT * KD], BF16, offset=R2_OFF + 48 * KB)
    sq_fm = AT("sq_fm", [128, NH * S], BF16, offset=R2_OFF)
    ostage = [AT(f"ostage{i}", [128, 512], F32, offset=R2_OFF + 2 * i * KB) for i in range(2)]
    xtile = [AT(f"xtile{i}", [128, 512], F32, offset=R2_OFF + (4 + 2 * i) * KB) for i in range(2)]
    wostr = [AT(f"wostr{i}", [128, D], BF16, offset=R2_OFF + (8 + 4 * i) * KB) for i in range(2)]
    xs_row = AT("xs_row", [1, S], F32, offset=R2_OFF + 16 * KB)
    xs_b = AT("xs_b", [128, S], F32, offset=R2_OFF + 24 * KB)
    fstr = [AT(f"fstr{i}", [128, D], BF16, offset=R2_OFF + (32 + 4 * i) * KB) for i in range(2)]
    ysa = [AT(f"ysa{i}", [128, 512], F32, offset=R2_OFF + (40 + 2 * i) * KB) for i in range(2)]
    ysb = [AT(f"ysb{i}", [128, 512], F32, offset=R2_OFF + (44 + 2 * i) * KB) for i in range(2)]
    xsr_sb = AT("xsr_sb", [16, 128], F32, offset=R2_OFF + 48 * KB)
    ostg0 = [AT(f"ostg0_{i}", [128, 512], F32, offset=R2_OFF + (49 + 2 * i) * KB) for i in range(2)]
    ystg0 = [AT(f"ystg0_{i}", [128, 512], F32, offset=R2_OFF + (53 + 2 * i) * KB) for i in range(2)]
    xA = [AT(f"xA{i}", [128, D], F32, offset=R3_OFF + 8 * i * KB) for i in range(2)]
    xc = [AT(f"xc{i}", [128, D], F32, offset=R3_OFF + (16 + 8 * i) * KB) for i in range(2)]
    xntm = [AT(f"xntm{i}", [128, D], BF16, offset=R3_OFF + (32 + 4 * i) * KB) for i in range(2)]
    qbf = [AT(f"qbf{i}", [128, D], BF16, offset=R3_OFF + (40 + 4 * i) * KB) for i in range(2)]
    qtmp = AT("qtmp", [128, 1024], F32, offset=R3_OFF + 48 * KB)
    qbfj = AT("qbfj", [128, D], BF16, offset=R3_OFF + 52 * KB)
    hbuf = [AT(f"hbuf{i}", [128, 1024], F32, offset=R3_OFF + (56 + 4 * i) * KB) for i in range(2)]
    qstr = [AT(f"qstr{i}", [128, D], BF16, offset=R3_OFF + 4 * i * KB) for i in range(2)]
    sel_bf = AT("sel_bf", [16, D], BF16, offset=R3_OFF + 8 * KB)
    rb_sb = [AT(f"rb{i}", [128, 512], F32, offset=R3_OFF + (12 + 2 * i) * KB) for i in range(2)]
    xq_fm = AT("xq_fm", [128, C * S], BF16, offset=R3_OFF)

    Pt = nc.alloc_psum_tensor("Pt", [128, 4096], F32)
    pb = [Pt[:, i * 512:(i + 1) * 512] for i in range(8)]
    Ptb = Pt.bitcast(BF16)
    pbb = [Ptb[:, i * 1024: i * 1024 + 128] for i in range(8)]
    banks = Banks(em, pb)

    def pslice(b, n):
        return Pt[:, b * 512: b * 512 + n]

    _echo_cache = {}

    def echo_mm(nmm):
        if nmm in _echo_cache:
            return _echo_cache[nmm]
        n = em.op("scalar", lambda e: e.nop(), inc="ad", waits=[("mm", nmm)])
        _echo_cache[nmm] = n
        return n
    sem_names = ["lc", "lx0", "lx1", "lwn0", "lwn1", "sxq0", "sxq1",
                 "swq0", "swq1", "lq0", "lq1", "lo0", "lo1", "lf0", "lf1",
                 "lxt0", "lxt1", "lxq", "lxn", "sxn0", "sxn1", "lsel", "sxr", "lxr",
                 "sob0", "sob1",
                 "mm", "vd", "ad", "gc"]

    em.dma("gpsimd", ident[:], dident, "lc")
    em.dma("gpsimd", bq_t[:], dbq, "lc")
    em.dma("gpsimd", fb_t[:], dfb, "lc")
    em.dma("gpsimd", bk_r[:], dbk, "lc")
    em.dma("gpsimd", bv_r[:], dbv, "lc")
    em.dma("gpsimd", wkT[:], dwk, "lc")
    n_lc = em.dma("gpsimd", wvT[:], dwv, "lc")
    n_bq = n_lc
    em.wait("scalar", "lc", n_lc)
    em.wait("tensor", "lc", n_lc)
    em.op("vector", lambda e: e.memset(ones_rb[:], 1.0))
    em.op("vector", lambda e: e.memset(ones_cb[:], 1.0))
    em.op("vector", lambda e: e.memset(ones_rf[:], 1.0))
    em.op("vector", lambda e: e.memset(kv_sb[:], 0.0))
    n_kem0 = em.op("vector", lambda e: e.memset(kemb[:], 0.0), inc="vd")
    em.op("vector", lambda e: e.memset(cns0[:], 1e-6))
    em.op("vector", lambda e: e.memset(mgc[:], MAGIC))
    n_idb = em.op("vector", lambda e: e.tensor_copy(ident_bf[:], ident[:]),
                  inc="vd", waits=[("lc", n_lc)])
    em.wait("tensor", "vd", n_idb)
    n_cns = em.op("vector", lambda e: e.memset(cns1[:], -0.5), inc="vd")
    for i in range(16):
        em.op("vector", lambda e, i=i: e.tensor_copy(bqc[i][:], bq_t[:, i:i + 1]),
              waits=[("lc", n_bq)])
        em.op("vector", lambda e, i=i: e.tensor_copy(fbc[i][:], fb_t[:, i:i + 1]))
    n_bqc = em.op("vector", lambda e: e.tensor_copy(cns0[:], cns0[:]), inc="vd")
    n_ones = em.op("vector", lambda e: e.memset(st["t1"][:, 0:1], 0.0), inc="vd")
    n_bq1 = em.op("vector", lambda e: e.tensor_scalar_add(bq1_t[:], bq_t[:], 1.0),
                  inc="vd", waits=[("lc", n_bq)])

    xA_g = [None, None]
    xc_g = [None, None]
    qbf_g = [None, None]
    xntm_g = [None, None]
    rc_g = [None, None]
    rwc_g = [None, None]
    e_g = [None, None]
    rp_g = [None, None]
    hbuf_g = [None, None]
    qtmp_g = [None]
    qbfj_g = [None]
    wqst_g = [None] * 4

    sk_ready = [0] * NT
    v_ready = [0] * NT
    lxn_slab = [0] * TG

    def chunk_stats(t):
        sl = t % 2
        n_ld = em.dma("sync", xA[sl][:], dx[t * 128:(t + 1) * 128, :],
                      f"lx{sl}", waits=[xA_g[sl]])
        n_sum = em.op("scalar", lambda e, sl=sl, t=t: e.activation(
            xc[sl][:], xA[sl][:], Act.Identity, accum_out=st["sum"][:, t:t + 1]),
            inc="ad", waits=[(f"lx{sl}", n_ld), xc_g[sl]])
        em.wait("vector", "ad", n_sum)
        em.vop(lambda e, t=t: e.tensor_scalar_mul(
            st["mean"][:, t:t + 1], st["sum"][:, t:t + 1], INV2048))
        em.vop(lambda e, t=t: e.tensor_scalar_mul(
            st["mbias"][:, t:t + 1], st["mean"][:, t:t + 1], -1.0))
        n_mb = em.vop(lambda e, sl=sl, t=t: e.tensor_copy(
            mbc[sl][:], st["mbias"][:, t:t + 1]), waits=[xc_g[sl]])
        n_xc = em.op("scalar", lambda e, sl=sl, t=t: e.activation(
            xc[sl][:], xA[sl][:], Act.Identity, bias=mbc[sl][:],
            scale=1.0), inc="ad", waits=[(f"lx{sl}", n_ld), ("vd", n_mb), xc_g[sl]])
        n_sq = em.op("scalar", lambda e, sl=sl, t=t: e.activation(
            xA[sl][:], xc[sl][:], Act.Square, accum_out=st["ssq"][:, t:t + 1]),
            inc="ad")
        xA_g[sl] = [("ad", n_sq)]
        n_am = em.vop(lambda e, sl=sl, t=t: e.reduce_max(
            st["amax"][:, t:t + 1], xc[sl][:], axis=AxX,
            apply_absolute_value=True), waits=[("ad", n_xc)])
        return n_xc, n_am, 0, n_sq

    def stats_tail(pr, am_max, sq_max):
        c4 = slice(2 * pr, 2 * pr + 2)
        V = "vector"
        em.wait(V, "ad", sq_max)
        em.vop(lambda e: e.tensor_scalar_mul(st["ve"][:, c4], st["ssq"][:, c4], INV2048))
        n_ve = em.vop(lambda e: e.tensor_scalar_add(st["ve"][:, c4], st["ve"][:, c4], 1e-5))
        n_ln = em.op("scalar", lambda e: e.activation(
            st["t1"][:, c4], st["ve"][:, c4], Act.Ln), inc="ad", waits=[("vd", n_ve)])
        n_sd = em.op("scalar", lambda e: e.activation(
            st["r"][:, c4], st["t1"][:, c4], Act.Exp, scale=cns1[:]),
            inc="ad", waits=[("vd", n_cns)])
        em.wait(V, "ad", n_sd)
        for _ in range(2):
            em.vop(lambda e: e.tensor_mul(st["t1"][:, c4], st["r"][:, c4], st["r"][:, c4]))
            em.vop(lambda e: e.tensor_mul(st["t1"][:, c4], st["t1"][:, c4], st["ve"][:, c4]))
            em.vop(lambda e: e.tensor_scalar(st["t1"][:, c4], st["t1"][:, c4],
                                             -0.5, 1.5, Alu.mult, Alu.add))
            em.vop(lambda e: e.tensor_mul(st["r"][:, c4], st["r"][:, c4], st["t1"][:, c4]))
        em.wait(V, "vd", am_max)
        em.vop(lambda e: e.tensor_mul(st["t2"][:, c4], st["amax"][:, c4], st["r"][:, c4]))
        em.vop(lambda e: e.tensor_scalar(st["xs"][:, c4], st["t2"][:, c4], INV127,
                                         1e-12, Alu.mult, Alu.max))
        em.vop(lambda e: e.reciprocal(st["rxs"][:, c4], st["xs"][:, c4]))
        n = em.vop(lambda e: e.tensor_mul(st["s3"][:, c4], st["r"][:, c4],
                                          st["rxs"][:, c4]))
        return n

    def emit_slab_kv(g_slab):
        for cci in range(C):
            em.dma("sync",
                   xn_fm[:, cci * S + g_slab * 512: cci * S + g_slab * 512 + 512],
                   d_xnsc[g_slab * 512:(g_slab + 1) * 512,
                          cci * 128:(cci + 1) * 128],
                   "lxn", waits=[("sxn0", em.cur("sxn0")),
                                 ("sxn1", em.cur("sxn1"))], transpose=True)
        lxn_slab[g_slab] = em.cur("lxn")
        for t in range(g_slab * 4, g_slab * 4 + 4):
            for kind in ("k", "v"):
                bi = banks.take()
                em.wait("tensor", "lxn", lxn_slab[g_slab])
                em.wait("tensor", "lc", n_lc)
                wT = wkT if kind == "k" else wvT
                for cc in range(C):
                    em.op("tensor", lambda e, bi=bi, cc=cc, t=t, wT=wT: e.matmul(
                        pb[bi][:], xn_fm[:, cc * S + t * 128: cc * S + t * 128 + 128],
                        wT[:, cc * KD:(cc + 1) * KD], start=(cc == 0), stop=False))
                em.wait("tensor", "vd", n_ones)
                br = bk_r if kind == "k" else bv_r
                em.op("tensor", lambda e, bi=bi, br=br: e.matmul(
                    pb[bi][:], ones_rb[:], br[:], start=False, stop=True),
                    inc="mm")
                nmm = em.cur("mm")
                if kind == "k":
                    p2 = t % 2
                    n_e = em.op("scalar", lambda e, bi=bi, p2=p2: e.activation(
                        e_bf[p2][:], pb[bi][:], Act.Exp), inc="ad",
                        waits=[("mm", nmm), e_g[p2]])
                    n_r = em.op("vector", lambda e, bi=bi, p2=p2: e.tensor_scalar(
                        rp_bf[p2][:], pb[bi][:], 1.0, 1.0, Alu.add, Alu.max),
                        inc="vd", waits=[("ad", n_e), rp_g[p2]])
                    banks.done(bi, "vd", n_r)
                    n_m = em.op("vector", lambda e, t=t, p2=p2: e.tensor_tensor(
                        sk_tm[:, t * KD:(t + 1) * KD], e_bf[p2][:], rp_bf[p2][:],
                        Alu.min), inc="vd", waits=[("vd", n_r)])
                    e_g[p2] = [("vd", n_m)]
                    rp_g[p2] = [("vd", n_m)]
                    sk_ready[t] = n_m
                else:
                    n_v = em.op("scalar", lambda e, bi=bi, t=t: e.activation(
                        v_tm[:, t * KD:(t + 1) * KD], pb[bi][:], Act.Identity),
                        inc="ad", waits=[("mm", nmm)])
                    banks.done(bi, "ad", n_v)
                    v_ready[t] = n_v

    def emit_wq(j):
        sl = j % 2
        n_lw = [0, 0]
        for h in range(2):
            n_lw[h] = em.dma("scalar", hbuf[h][:],
                             dwn[j * 128:(j + 1) * 128, h * 1024:(h + 1) * 1024],
                             f"lwn{h}", waits=[hbuf_g[h]])
        em.vop(lambda e, j=j: e.reduce_max(
            st["wmax"][:, j:j + 1], hbuf[0][:], axis=AxX,
            apply_absolute_value=True), waits=[("lwn0", n_lw[0])])
        em.vop(lambda e, j=j: e.reduce_max(
            st["wmaxb"][:, j:j + 1], hbuf[1][:], axis=AxX,
            apply_absolute_value=True), waits=[("lwn1", n_lw[1])])
        em.vop(lambda e, j=j: e.tensor_tensor(
            st["wmax"][:, j:j + 1], st["wmax"][:, j:j + 1],
            st["wmaxb"][:, j:j + 1], Alu.max))
        em.vop(lambda e, j=j: e.tensor_scalar(
            st["wsc"][:, j:j + 1], st["wmax"][:, j:j + 1], 1e-8, INV127,
            Alu.max, Alu.mult))
        em.vop(lambda e, j=j: e.tensor_copy(wscc[j][:], st["wsc"][:, j:j + 1]))
        em.vop(lambda e, j=j: e.reciprocal(
            st["rws"][:, j:j + 1], st["wsc"][:, j:j + 1]))
        n_rw = em.vop(lambda e, sl=sl, j=j: e.tensor_copy(
            rwc[sl][:], st["rws"][:, j:j + 1]), waits=[rwc_g[sl]])
        for h in range(2):
            n_sc = em.op("scalar", lambda e, h=h, sl=sl: e.activation(
                qtmp[:], hbuf[h][:], Act.Identity, scale=rwc[sl][:],
                bias=mgc[:]), inc="ad",
                waits=[("vd", n_rw), (f"lwn{h}", n_lw[h]), qtmp_g[0]])
            hbuf_g[h] = [("ad", n_sc)]
            n_rnd = em.op("vector", lambda e, h=h: e.tensor_scalar(
                qbfj[:, h * 1024:(h + 1) * 1024], qtmp[:], -MAGIC, None, Alu.add),
                inc="vd", waits=[("ad", n_sc), (qbfj_g[0] if h == 0 else None)])
            qtmp_g[0] = [("vd", n_rnd)]
        rwc_g[sl] = [("ad", em.cur("ad"))]
        n_rnd_all = em.cur("vd")
        last_mm = 0
        for cc in range(C):
            bi = banks.take()
            em.wait("tensor", "vd", n_rnd_all)
            nmm = em.op("tensor", lambda e, bi=bi, cc=cc: e.transpose(
                pbb[bi][:], qbfj[:, cc * 128:(cc + 1) * 128],
                ident_bf[:]), inc="mm")
            ri = cc % 4
            n_ev = em.op("scalar", lambda e, bi=bi, ri=ri: e.activation(
                wqT_st[ri][:], pbb[bi][:], Act.Identity), inc="ad",
                waits=[("mm", nmm), wqst_g[ri]])
            banks.done(bi, "ad", n_ev)
            n_st = em.dma("gpsimd",
                          d_wq2[(cc * 16 + j) * 128:(cc * 16 + j) * 128 + 128, :],
                          wqT_st[ri][:], f"swq{ri % 2}", waits=[("ad", n_ev)])
            wqst_g[ri] = [(f"swq{ri % 2}", n_st)]
            last_mm = nmm
        qbfj_g[0] = [("mm", last_mm)]

    n_stats_last = 0
    for pr in range(NT // 2):
        tg = pr // 2
        ts = (2 * pr, 2 * pr + 1)
        per = {t: chunk_stats(t) for t in ts}
        n_st = stats_tail(pr, max(per[t][1] for t in ts),
                          max(per[t][3] for t in ts))
        n_stats_last = n_st
        for t in ts:
            sl = t % 2
            n_rc = em.vop(lambda e, sl=sl, t=t: e.tensor_copy(
                rc[sl][:], st["r"][:, t:t + 1]), waits=[("vd", n_st), rc_g[sl]])
            n_xn = em.op("scalar", lambda e, sl=sl: e.activation(
                xntm[sl][:], xc[sl][:], Act.Identity, scale=rc[sl][:]),
                inc="ad", waits=[("vd", n_rc), xntm_g[sl]])
            rc_g[sl] = [("ad", n_xn)]
            n_s = em.dma("gpsimd", d_xnsc[t * 128:(t + 1) * 128, :],
                         xntm[sl][:], f"sxn{sl}", waits=[("ad", n_xn)])
            xntm_g[sl] = [(f"sxn{sl}", n_s)]
            n_q1 = em.op("vector", lambda e, sl=sl, t=t: e.tensor_scalar(
                xA[sl][:], xc[sl][:], st["s3"][:, t:t + 1], MAGIC,
                Alu.mult, Alu.add),
                inc="vd", waits=[("vd", n_st), xA_g[sl]])
            bsl = t % 2
            n_q2 = em.op("vector", lambda e, sl=sl, bsl=bsl: e.tensor_scalar(
                qbf[bsl][:], xA[sl][:], -MAGIC, None, Alu.add),
                inc="vd", waits=[("vd", n_q1), qbf_g[bsl]])
            xA_g[sl] = [("vd", n_q2)]
            n_sq_ = em.dma("gpsimd", d_xqsc[t * 128:(t + 1) * 128, :],
                           qbf[bsl][:], f"sxq{bsl}", waits=[("vd", n_q2)])
            qbf_g[bsl] = [(f"sxq{bsl}", n_sq_)]
            xc_g[sl] = [("vd", em.cur("vd")), ("ad", em.cur("ad"))]

        if pr % 2 == 1:
            g_pend = pr // 2
            if g_pend >= 1:
                emit_slab_kv(g_pend - 1)
            if pr == NT // 2 - 1:
                emit_slab_kv(g_pend)

    kvb = [banks.take() for _ in range(4)]
    ksb = [banks.take() for _ in range(4)]
    for b in kvb + ksb:
        banks.pin(b)
    for t in range(NT):
        em.wait("tensor", "vd", sk_ready[t])
        em.wait("tensor", "ad", v_ready[t])
        for gk in range(GK):
            em.op("tensor", lambda e, gk=gk, t=t: e.matmul(
                pslice(kvb[gk], 128),
                sk_tm[:, t * KD + gk * 128: t * KD + gk * 128 + 128],
                v_tm[:, t * KD + gk * 128: t * KD + gk * 128 + 128],
                start=(t == 0), stop=(t == NT - 1), skip_group_check=True))
            em.op("tensor", lambda e, gk=gk, t=t: e.matmul(
                pslice(ksb[gk], 1),
                sk_tm[:, t * KD + gk * 128: t * KD + gk * 128 + 128],
                ones_cb[:], start=(t == 0), stop=(t == NT - 1),
                skip_group_check=True),
                inc=("mm" if (t == NT - 1 and gk == GK - 1) else None))
    n_kvmm = em.cur("mm")
    n_kvec = echo_mm(n_kvmm)
    n_kv = [0] * GK
    for gk in range(GK):
        em.op("vector", lambda e, gk=gk: e.tensor_copy(
            kv_sb[:, gk * 132: gk * 132 + 128], pslice(kvb[gk], 128)),
            waits=[("ad", n_kvec)])
        n_kv[gk] = em.op("vector", lambda e, gk=gk: e.tensor_copy(
            kv_sb[:, gk * 132 + 128: gk * 132 + 129], pslice(ksb[gk], 1)),
            inc="vd", waits=[("ad", n_kvec)])
    for gk in range(GK):
        banks.unpin(kvb[gk], "vd", n_kv[gk])
        banks.unpin(ksb[gk], "vd", n_kv[gk])
    n_kemb = 0
    for h in range(NH):
        gk = h // 4
        n_kemb = em.op("vector", lambda e, h=h, gk=gk: e.tensor_copy(
            kemb[:, h * 16 + h: h * 16 + h + 1],
            kv_sb[:, gk * 132 + 128: gk * 132 + 129]), inc="vd",
            waits=[("vd", max(n_kv[gk], n_kem0))])

    n_A_ad = em.cur("ad")
    n_A_vd = em.cur("vd")
    n_A_st = [("sxn0", em.cur("sxn0")), ("sxn1", em.cur("sxn1")),
              ("sxq0", em.cur("sxq0")), ("sxq1", em.cur("sxq1"))]
    A_dead = [("ad", n_A_ad), ("vd", n_A_vd)] + n_A_st

    n_sel = em.dma("gpsimd", sel_bf[:], dsel, "lsel", waits=[A_dead])

    lxn_all = em.cur("lxn")
    qstr_g = [A_dead, A_dead]
    hbuf_g[0] = A_dead
    hbuf_g[1] = A_dead
    qtmp_g[0] = A_dead
    qbfj_g[0] = A_dead
    for i in range(NH):
        sl = i % 2
        n_ld = em.dma("gpsimd", qstr[sl][:], dwq[i * 128:(i + 1) * 128, :],
                      f"lq{sl}", waits=[qstr_g[sl]])
        for g in range(TG):
            bi = banks.take()
            em.wait("tensor", f"lq{sl}", n_ld)
            em.wait("tensor", "lxn", lxn_all)
            for cc in range(C):
                em.op("tensor", lambda e, bi=bi, sl=sl, cc=cc, g=g: e.matmul(
                    pb[bi][:], qstr[sl][:, cc * 128:(cc + 1) * 128],
                    xn_fm[:, cc * S + g * 512: cc * S + g * 512 + 512],
                    start=(cc == 0), stop=(cc == C - 1)),
                    inc=("mm" if cc == C - 1 else None))
            nmm = em.cur("mm")
            p2 = (i * TG + g) % 2
            n_e = em.op("scalar", lambda e, bi=bi, i=i, p2=p2: e.activation(
                e_bf[p2][:], pb[bi][:], Act.Exp, bias=bqc[i][:]),
                inc="ad", waits=[("mm", nmm), ("vd", n_bqc), e_g[p2]])
            n_r = em.op("vector", lambda e, bi=bi, i=i, p2=p2: e.tensor_scalar(
                rp_bf[p2][:], pb[bi][:], bq1_t[:, i:i + 1], 1.0, Alu.add, Alu.max),
                inc="vd", waits=[("ad", n_e), ("vd", n_bq1), rp_g[p2]])
            banks.done(bi, "vd", n_r)
            n_m = em.op("vector", lambda e, i=i, g=g, p2=p2: e.tensor_tensor(
                sq_fm[:, i * S + g * 512: i * S + g * 512 + 512],
                e_bf[p2][:], rp_bf[p2][:], Alu.min), inc="vd",
                waits=[("vd", n_r)])
            e_g[p2] = [("vd", n_m)]
            rp_g[p2] = [("vd", n_m)]
        emit_wq(i)
        qstr_g[sl] = [("mm", em.cur("mm"))]
    n_qproj_mm = em.cur("mm")
    n_sq_done = em.cur("vd")
    wq_dead = [("vd", em.cur("vd")), ("ad", em.cur("ad")),
               ("mm", em.cur("mm"))]

    xq_ld_n = [0] * C

    def load_xq(cc, eng, extra):
        xq_ld_n[cc] = em.dma(
            eng, xq_fm[:, cc * S: cc * S + S],
            d_xqsc[:, cc * 128:(cc + 1) * 128], "lxq",
            waits=[A_dead] + list(extra), transpose=True)

    for cc in range(4, 12):
        load_xq(cc, "sync", [])
    for cc in range(12, 16):
        load_xq(cc, "sync",
                wq_dead + [("swq0", em.cur("swq0")), ("swq1", em.cur("swq1"))])

    rb_g = [A_dead, A_dead]
    attn_g = [0] * TG
    rec_guard = None
    for g in range(TG):
        if g == 1:
            load_xq(0, "sync", [("mm", n_qproj_mm)])
            load_xq(1, "sync", [("mm", n_qproj_mm)])
        bi_n = banks.take()
        em.wait("tensor", "vd", n_sq_done)
        em.wait("tensor", "vd", n_kemb)
        banks.pin(bi_n)
        for h in range(NH):
            em.op("tensor", lambda e, bi=bi_n, h=h, g=g: e.matmul(
                pb[bi][0:16, 0:512], kemb[:, h * 16:(h + 1) * 16],
                sq_fm[:, h * S + g * 512: h * S + g * 512 + 512],
                start=(h == 0), stop=(h == NH - 1)),
                inc=("mm" if h == NH - 1 else None))
        n_nmm = em.cur("mm")
        n_nec = echo_mm(n_nmm)
        n_re = em.op("vector", lambda e, bi=bi_n: e.tensor_scalar(
            pb[bi][32:48, 0:512], pb[bi][0:16, 0:512], 1e-6, None, Alu.add),
            inc="vd", waits=[("ad", n_nec)])
        n_rf = em.op("vector", lambda e, bi=bi_n: e.reciprocal(
            pb[bi][0:16, 0:512], pb[bi][32:48, 0:512]), inc="vd",
            waits=[("vd", n_re)])
        n_rb16 = em.op("vector", lambda e, bi=bi_n: e.tensor_copy(
            rec_bf[:], pb[bi][0:16, 0:512]), inc="vd",
            waits=[("vd", n_rf), rec_guard])
        banks.unpin(bi_n, "vd", n_rb16)
        for h in range(NH):
            p2 = h % 2
            bi_r = banks.take()
            em.wait("tensor", "vd", n_rb16)
            em.wait("tensor", "lsel", n_sel)
            n_rmm = em.op("tensor", lambda e, bi=bi_r, h=h: e.matmul(
                pb[bi][:], sel_bf[:, h * 128:(h + 1) * 128], rec_bf[:],
                start=True, stop=True), inc="mm")
            bi_x = banks.take()
            gk = h // 4
            em.wait("tensor", "vd", n_kv[gk])
            n_xmm = em.op("tensor", lambda e, bi=bi_x, h=h, g=g, gk=gk: e.matmul(
                pb[bi][:], kv_sb[:, gk * 132: gk * 132 + 128],
                sq_fm[:, h * S + g * 512: h * S + g * 512 + 512],
                start=True, stop=True), inc="mm")
            n_rbe = em.op("scalar", lambda e, bi=bi_r, p2=p2: e.activation(
                rb_sb[p2][:], pb[bi][:], Act.Identity), inc="ad",
                waits=[("mm", n_xmm), rb_g[p2]])
            banks.done(bi_r, "ad", n_rbe)
            n_div = em.op("vector", lambda e, bi=bi_x, h=h, g=g, p2=p2: e.tensor_tensor(
                attn_fm[:, h * S + g * 512: h * S + g * 512 + 512],
                pb[bi][:], rb_sb[p2][:], Alu.mult), inc="vd",
                waits=[("ad", n_rbe)])
            banks.done(bi_x, "vd", n_div)
            rb_g[p2] = [("vd", n_div)]
        rec_guard = [("ad", echo_mm(em.cur("mm")))]
        attn_g[g] = em.cur("vd")
    n_pd_mm = em.cur("mm")
    n_attn_vd = em.cur("vd")

    load_xq(2, "sync", [("mm", n_pd_mm)])
    load_xq(3, "sync", [("vd", n_attn_vd)])
    xq_all = em.cur("lxq")

    bi_t = banks.take()
    n_xst = em.op("tensor", lambda e, bi=bi_t: e.transpose(
        pb[bi][0:16, 0:128], st["xs"][:, 0:16], ident[:]), inc="mm",
        waits=[("vd", n_stats_last)])
    n_xsr = em.op("scalar", lambda e, bi=bi_t: e.activation(
        xsr_sb[:], pb[bi][0:16, 0:128], Act.Identity), inc="ad",
        waits=[("mm", max(n_xst, n_pd_mm))])
    banks.done(bi_t, "ad", n_xsr)
    n_xsst = em.dma("gpsimd", d_xsr.rearrange("a (b c) -> (a b) c", b=16),
                    xsr_sb[:], "sxr", waits=[("ad", n_xsr)])
    n_xrow = em.dma("gpsimd", xs_row[:], d_xsr, "lxr",
                    waits=[("sxr", n_xsst), ("mm", n_pd_mm)])
    xsb_ev = 0
    for g in range(TG):
        bi = banks.take()
        em.wait("tensor", "lxr", n_xrow)
        nmm = em.op("tensor", lambda e, bi=bi, g=g: e.matmul(
            pb[bi][:], ones_rf[:], xs_row[:, g * 512:(g + 1) * 512],
            start=True, stop=True), inc="mm")
        xsb_ev = em.op("scalar", lambda e, bi=bi, g=g: e.activation(
            xs_b[:, g * 512:(g + 1) * 512], pb[bi][:], Act.Identity),
            inc="ad", waits=[("mm", nmm)])
        banks.done(bi, "ad", xsb_ev)

    wostr_g = [None, None]
    fstr_g = [None, None]
    xt_g = [None, None]
    ost_g = [None, None]
    ystg_g = [None, None]
    ya_g = [None, None]
    yb_g = [None, None]
    ostage_g = [None, None]
    n_swq = [("swq0", em.cur("swq0")), ("swq1", em.cur("swq1"))]
    for i in range(C):
        sl = i % 2
        n_wol = em.dma("gpsimd", wostr[sl][:], dwo[i * 128:(i + 1) * 128, :],
                       f"lo{sl}", waits=[("mm", n_pd_mm), wostr_g[sl]])
        n_fl = 0
        for cc in range(C):
            n_fl = em.dma("sync", fstr[sl][:, cc * 128:(cc + 1) * 128],
                          d_wq2[(cc * 16 + i) * 128:(cc * 16 + i) * 128 + 128, :],
                          f"lf{sl}",
                          waits=[("mm", n_pd_mm)] + n_swq + [fstr_g[sl]])
        for g in range(TG):
            xsl = (i * TG + g) % 2
            n_xl = em.dma("scalar", xtile[xsl][:],
                          dxp[(i * TG + g) * 128:(i * TG + g) * 128 + 128, :],
                          f"lxt{xsl}", waits=[("mm", n_pd_mm), xt_g[xsl]])
            biA = banks.take()
            em.wait("tensor", f"lo{sl}", n_wol)
            em.wait("tensor", "vd", attn_g[g])
            for cc in range(C):
                em.op("tensor", lambda e, bi=biA, sl=sl, cc=cc, g=g: e.matmul(
                    pb[bi][:], wostr[sl][:, cc * 128:(cc + 1) * 128],
                    attn_fm[:, cc * S + g * 512: cc * S + g * 512 + 512],
                    start=(cc == 0), stop=(cc == C - 1)),
                    inc=("mm" if cc == C - 1 else None))
            nmmA = em.cur("mm")
            biB = banks.take()
            em.wait("tensor", f"lf{sl}", n_fl)
            em.wait("tensor", "lxq", xq_all)
            for cc in range(C):
                em.op("tensor", lambda e, bi=biB, sl=sl, cc=cc, g=g: e.matmul(
                    pb[bi][:], fstr[sl][:, cc * 128:(cc + 1) * 128],
                    xq_fm[:, cc * S + g * 512: cc * S + g * 512 + 512],
                    start=(cc == 0), stop=(cc == C - 1)),
                    inc=("mm" if cc == C - 1 else None))
            nmmB = em.cur("mm")
            osl = (i * TG + g) % 2
            n_evA = em.op("scalar", lambda e, bi=biA, osl=osl, i=i: e.activation(
                ostg0[osl][:], pb[bi][:], Act.Identity, bias=fbc[i][:]),
                inc="ad", waits=[("mm", nmmA), ost_g[osl]])
            banks.done(biA, "ad", n_evA)
            n_evB = em.op("scalar", lambda e, bi=biB, osl=osl, i=i: e.activation(
                ystg0[osl][:], pb[bi][:], Act.Identity, scale=wscc[i][:]),
                inc="ad", waits=[("mm", nmmB), ystg_g[osl]])
            banks.done(biB, "ad", n_evB)
            n_y1 = em.op("vector", lambda e, osl=osl, g=g: e.tensor_tensor(
                ysa[osl][:], ystg0[osl][:], xs_b[:, g * 512:(g + 1) * 512],
                Alu.mult), inc="vd",
                waits=[("ad", n_evB), ("ad", xsb_ev), ya_g[osl]])
            ystg_g[osl] = [("vd", n_y1)]
            n_y2 = em.op("vector", lambda e, osl=osl: e.tensor_tensor(
                ysb[osl][:], ysa[osl][:], ostg0[osl][:], Alu.add), inc="vd",
                waits=[("ad", n_evA), yb_g[osl]])
            ya_g[osl] = [("vd", n_y2)]
            ost_g[osl] = [("vd", n_y2)]
            n_y3 = em.op("vector", lambda e, osl=osl, xsl=xsl: e.tensor_tensor(
                ostage[osl][:], ysb[osl][:], xtile[xsl][:], Alu.add), inc="vd",
                waits=[(f"lxt{xsl}", n_xl), ostage_g[osl]])
            yb_g[osl] = [("vd", n_y3)]
            xt_g[xsl] = [("vd", n_y3)]
            n_os = em.dma("gpsimd",
                          dout[i * 128:(i + 1) * 128, g * 512:(g + 1) * 512],
                          ostage[osl][:], f"sob{osl}", waits=[("vd", n_y3)])
            ostage_g[osl] = [(f"sob{osl}", n_os)]
        wostr_g[sl] = [("mm", em.cur("mm"))]
        fstr_g[sl] = [("mm", em.cur("mm"))]

    return _finish(nc, em, sem_names)


def _finish(nc, em, sem_names):
    for sname in ("sob0", "sob1"):
        em.wait("sync", sname, em.cur(sname))
        em.wait("gpsimd", sname, em.cur(sname))
    for sname in ("sxq0", "sxq1", "swq0", "swq1", "sxr", "sxn0", "sxn1"):
        em.wait("gpsimd", sname, em.cur(sname))
    em.wait("vector", "mm", em.cur("mm"))
    em.wait("scalar", "mm", em.cur("mm"))

    for s in sem_names:
        em.sems[s] = nc.alloc_semaphore(s)
    with nc.Block() as block:

        @block.sync
        def _(e):
            for f in em.scripts["sync"]:
                f(e)

        @block.tensor
        def _(e):
            for f in em.scripts["tensor"]:
                f(e)

        @block.vector
        def _(e):
            for f in em.scripts["vector"]:
                f(e)

        @block.scalar
        def _(e):
            for f in em.scripts["scalar"]:
                f(e)

        @block.gpsimd
        def _(e):
            for f in em.scripts["gpsimd"]:
                f(e)

    return nc


def make_inputs(b, x, Wq, bq, Wk, bk, Wv, bv, Wo, Wf, fb):
    f32 = np.float32
    xb = np.ascontiguousarray(x[b], dtype=f32)
    x_pk = np.ascontiguousarray(
        xb.T.reshape(C, 128, TG, 512).transpose(0, 2, 1, 3)).reshape(C * TG * 128, 512)
    wq_pk = np.ascontiguousarray(
        Wq.reshape(NH, 128, C, 128).transpose(0, 3, 2, 1)).reshape(NH * 128, D)
    wo_pk = np.ascontiguousarray(
        Wo.reshape(C, 128, C, 128).transpose(0, 3, 2, 1)).reshape(C * 128, D)
    wk_pk = np.ascontiguousarray(
        Wk.T.reshape(C, 128, KD).transpose(1, 0, 2)).reshape(128, C * KD)
    wv_pk = np.ascontiguousarray(
        Wv.T.reshape(C, 128, KD).transpose(1, 0, 2)).reshape(128, C * KD)
    sel = np.zeros((16, D), f32)
    for h in range(16):
        sel[h, h * 128:(h + 1) * 128] = 1.0
    return {
        "x_nat": xb,
        "x_pk": x_pk,
        "wq_pk": wq_pk,
        "wo_pk": wo_pk,
        "wk_pk": wk_pk,
        "wv_pk": wv_pk,
        "wn_nat": np.ascontiguousarray(Wf, dtype=f32),
        "bq_t": np.ascontiguousarray(bq.reshape(16, 128).T),
        "fb_t": np.ascontiguousarray(fb.reshape(16, 128).T),
        "bk_r": np.ascontiguousarray(bk[None, :]),
        "bv_r": np.ascontiguousarray(bv[None, :]),
        "sel_h": sel,
        "ident": np.eye(128, dtype=f32),
    }


def kernel(L0_attn_hidden_states, L0_attn_Wq, L0_attn_bq, L0_attn_Wk,
           L0_attn_bk, L0_attn_Wv, L0_attn_bv, L0_attn_Wo, L0_ffn_weight,
           L0_ffn_bias):
    x = np.asarray(L0_attn_hidden_states, dtype=np.float32)
    Wq = np.asarray(L0_attn_Wq, dtype=np.float32)
    bq = np.asarray(L0_attn_bq, dtype=np.float32)
    Wk = np.asarray(L0_attn_Wk, dtype=np.float32)
    bk = np.asarray(L0_attn_bk, dtype=np.float32)
    Wv = np.asarray(L0_attn_Wv, dtype=np.float32)
    bv = np.asarray(L0_attn_bv, dtype=np.float32)
    Wo = np.asarray(L0_attn_Wo, dtype=np.float32)
    Wf = np.asarray(L0_ffn_weight, dtype=np.float32)
    fb = np.asarray(L0_ffn_bias, dtype=np.float32)
    nc = build_nc()
    in_maps = [make_inputs(b, x, Wq, bq, Wk, bk, Wv, bv, Wo, Wf, fb)
               for b in range(8)]
    res = run_bass_kernel_spmd(nc, in_maps, core_ids=list(range(8)))
    out = np.stack([np.ascontiguousarray(r["out"].T) for r in res.results])
    return out


# version 13
# speedup vs baseline: 1.0328x; 1.0328x over previous
import numpy as np
import concourse.bass as bass
import concourse.mybir as mybir
from concourse.bass_utils import run_bass_kernel_spmd

F32 = mybir.dt.float32
BF16 = mybir.dt.bfloat16
I32 = mybir.dt.int32
Alu = mybir.AluOpType
Act = mybir.ActivationFunctionType
AxX = mybir.AxisListType.X

S = 2048
D = 2048
C = 16
NT = 16
TG = 4
NH = 16
GK = 4
KD = 512

R1_OFF = 32704
R2_OFF = 98240
R3_OFF = 163776
KB = 1024

INV2048 = float(np.float32(1.0 / 2048.0))
INV127 = float(np.float32(1.0 / 127.0))
MAGIC = 12582912.0


class Emit:
    ENGS = ("sync", "tensor", "vector", "scalar", "gpsimd")

    def __init__(self):
        self.scripts = {e: [] for e in self.ENGS}
        self.ctr = {}
        self.waited = {e: {} for e in self.ENGS}
        self.sems = {}

    def inc_val(self, sem, n):
        self.ctr[sem] = self.ctr.get(sem, 0) + n
        return self.ctr[sem]

    def cur(self, sem):
        return self.ctr.get(sem, 0)

    def wait(self, eng, sem, val):
        if val is None or val <= 0:
            return
        if self.waited[eng].get(sem, 0) >= val:
            return
        self.waited[eng][sem] = val
        self.scripts[eng].append(
            lambda e, s=sem, v=val: e.wait_ge(self.sems[s], v))

    def waits(self, eng, ws):
        for w in ws:
            if w is None:
                continue
            if isinstance(w, list):
                self.waits(eng, w)
            else:
                self.wait(eng, w[0], w[1])

    def op(self, eng, fn, inc=None, waits=()):
        self.waits(eng, waits)
        if inc is None:
            self.scripts[eng].append(fn)
            return None
        n = self.inc_val(inc, 1)
        self.scripts[eng].append(
            lambda e, f=fn, s=inc: f(e).then_inc(self.sems[s], 1))
        return n

    def vop(self, fn, waits=()):
        n = self.op("vector", fn, inc="vd", waits=waits)
        self.wait("vector", "vd", n)
        return n

    def dma(self, eng, out, in_, sem, waits=(), **kw):
        self.waits(eng, waits)
        n = self.inc_val(sem, 16)
        self.scripts[eng].append(
            lambda e, o=out, i=in_, s=sem, kw=kw: e.dma_start(out=o, in_=i, **kw)
            .then_inc(self.sems[s], 16))
        return n


class Banks:
    def __init__(self, em, pb):
        self.em = em
        self.pb = pb
        self.last = [None] * len(pb)
        self.rr = 0
        self.pinned = set()

    def take(self, n=1):
        nb = len(self.pb)
        for _ in range(nb):
            s = self.rr % nb
            self.rr += 1
            if s + n > nb:
                continue
            if any((s + k) in self.pinned for k in range(n)):
                continue
            for k in range(n):
                if self.last[s + k] is not None:
                    self.em.wait("tensor", self.last[s + k][0], self.last[s + k][1])
            self.rr = s + n
            return s
        raise RuntimeError("no free psum bank group")

    def done(self, i, sem, cnt):
        self.last[i] = (sem, cnt)

    def pin(self, i):
        self.pinned.add(i)
        if self.last[i] is not None:
            self.em.wait("tensor", self.last[i][0], self.last[i][1])

    def unpin(self, i, sem, cnt):
        self.pinned.discard(i)
        self.last[i] = (sem, cnt)


def build_nc():
    nc = bass.Bass("TRN2", target_bir_lowering=False,
                   detect_race_conditions=False)
    em = Emit()

    dx = nc.dram_tensor("x_nat", [S, D], F32, kind="ExternalInput").ap()
    dxp = nc.dram_tensor("x_pk", [C * TG * 128, 512], F32, kind="ExternalInput").ap()
    dwq = nc.dram_tensor("wq_pk", [NH * 128, D], F32, kind="ExternalInput").ap()
    dwo = nc.dram_tensor("wo_pk", [C * 128, D], F32, kind="ExternalInput").ap()
    dwk = nc.dram_tensor("wk_pk", [128, C * KD], F32, kind="ExternalInput").ap()
    dwv = nc.dram_tensor("wv_pk", [128, C * KD], F32, kind="ExternalInput").ap()
    dwn = nc.dram_tensor("wn_nat", [D, D], F32, kind="ExternalInput").ap()
    dbq = nc.dram_tensor("bq_t", [128, 16], F32, kind="ExternalInput").ap()
    dfb = nc.dram_tensor("fb_t", [128, 16], F32, kind="ExternalInput").ap()
    dbk = nc.dram_tensor("bk_r", [1, KD], F32, kind="ExternalInput").ap()
    dbv = nc.dram_tensor("bv_r", [1, KD], F32, kind="ExternalInput").ap()
    dsel = nc.dram_tensor("sel_h", [16, D], F32, kind="ExternalInput").ap()
    dident = nc.dram_tensor("ident", [128, 128], F32, kind="ExternalInput").ap()
    dout = nc.dram_tensor("out", [D, S], F32, kind="ExternalOutput").ap()
    d_wq2 = nc.dram_tensor("wq_sc", [C * C * 128, 128], BF16, kind="Internal").ap()
    d_xqsc = nc.dram_tensor("xq_sc", [S, D], BF16, kind="Internal").ap()
    d_xnsc = nc.dram_tensor("xn_sc", [S, D], BF16, kind="Internal").ap()
    d_xsr = nc.dram_tensor("xsr_sc", [1, S], F32, kind="Internal").ap()

    A = nc.alloc_sbuf_tensor
    ident = A("sb_ident", [128, 128], F32)
    ident_bf = A("sb_identb", [128, 128], BF16)
    ones_rb = A("ones_rb", [1, 128], BF16)
    ones_cb = A("ones_cb", [128, 1], BF16)
    ones_rf = A("ones_rf", [1, 128], F32)
    stn = ["sum", "mean", "mbias", "ssq", "ve", "r", "amax", "xs", "rxs",
           "s3", "t1", "t2", "wmax", "wmaxb", "wsc", "rws"]
    st = {k: A("st_" + k, [128, 16], F32) for k in stn}
    bq_t = A("sb_bq", [128, 16], F32)
    bq1_t = A("bq1_t", [128, 16], F32)
    fb_t = A("sb_fb", [128, 16], F32)
    bk_r = A("sb_bk", [1, KD], BF16)
    bv_r = A("sb_bv", [1, KD], BF16)
    kv_sb = A("kv_sb", [128, 4 * 132], BF16)
    kemb = A("kemb", [128, 256], BF16)
    e_bf = [A(f"e_bf{i}", [128, 512], BF16) for i in range(2)]
    rp_bf = [A(f"rp_bf{i}", [128, 512], BF16) for i in range(2)]
    cns0 = A("cns0", [128, 1], F32)
    cns1 = A("cns1", [128, 1], F32)
    mgc = A("mgc", [128, 1], F32)
    bqc = [A(f"bqc{i}", [128, 1], F32) for i in range(16)]
    fbc = [A(f"fbc{i}", [128, 1], F32) for i in range(16)]
    wscc = [A(f"wscc{i}", [128, 1], F32) for i in range(16)]
    mbc = [A(f"mbc{i}", [128, 1], F32) for i in range(2)]
    rc = [A(f"rc{i}", [128, 1], F32) for i in range(2)]
    rwc = [A(f"rwc{i}", [128, 1], F32) for i in range(2)]
    wqT_st = [A(f"wqT{i}", [128, 128], BF16) for i in range(4)]
    rec_bf = A("rec_bf", [16, 512], BF16)
    assert nc.sbuf_base <= R1_OFF, f"persist overflow: {nc.sbuf_base}"

    AT = nc.alloc_sbuf_tensor_at
    xn_fm = AT("xn_fm", [128, C * S], BF16, offset=R1_OFF)
    attn_fm = AT("attn_fm", [128, C * S], BF16, offset=R1_OFF)
    wkT = AT("wkT", [128, C * KD], BF16, offset=R2_OFF)
    wvT = AT("wvT", [128, C * KD], BF16, offset=R2_OFF + 16 * KB)
    sk_tm = AT("sk_tm", [128, NT * KD], BF16, offset=R2_OFF + 32 * KB)
    v_tm = AT("v_tm", [128, NT * KD], BF16, offset=R2_OFF + 48 * KB)
    sq_fm = AT("sq_fm", [128, NH * S], BF16, offset=R2_OFF)
    ostage = [AT(f"ostage{i}", [128, 512], F32, offset=R2_OFF + 2 * i * KB) for i in range(2)]
    xtile = [AT(f"xtile{i}", [128, 512], F32, offset=R2_OFF + (4 + 2 * i) * KB) for i in range(2)]
    wostr = [AT(f"wostr{i}", [128, D], BF16, offset=R2_OFF + (8 + 4 * i) * KB) for i in range(2)]
    xs_row = AT("xs_row", [1, S], F32, offset=R2_OFF + 16 * KB)
    xs_b = AT("xs_b", [128, S], F32, offset=R2_OFF + 24 * KB)
    fstr = [AT(f"fstr{i}", [128, D], BF16, offset=R2_OFF + (32 + 4 * i) * KB) for i in range(2)]
    ysa = [AT(f"ysa{i}", [128, 512], F32, offset=R2_OFF + (40 + 2 * i) * KB) for i in range(2)]
    ysb = [AT(f"ysb{i}", [128, 512], F32, offset=R2_OFF + (44 + 2 * i) * KB) for i in range(2)]
    xsr_sb = AT("xsr_sb", [16, 128], F32, offset=R2_OFF + 48 * KB)
    ostg0 = [AT(f"ostg0_{i}", [128, 512], F32, offset=R2_OFF + (49 + 2 * i) * KB) for i in range(2)]
    ystg0 = [AT(f"ystg0_{i}", [128, 512], F32, offset=R2_OFF + (53 + 2 * i) * KB) for i in range(2)]
    xA = [AT(f"xA{i}", [128, D], F32, offset=R3_OFF + 8 * i * KB) for i in range(2)]
    xc = [AT(f"xc{i}", [128, D], F32, offset=R3_OFF + (16 + 8 * i) * KB) for i in range(2)]
    xntm = [AT(f"xntm{i}", [128, D], BF16, offset=R3_OFF + (32 + 4 * i) * KB) for i in range(2)]
    qbf = [AT(f"qbf{i}", [128, D], BF16, offset=R3_OFF + (40 + 4 * i) * KB) for i in range(2)]
    qtmp = AT("qtmp", [128, 1024], F32, offset=R3_OFF + 48 * KB)
    qbfj = AT("qbfj", [128, D], BF16, offset=R3_OFF + 52 * KB)
    hbuf = [AT(f"hbuf{i}", [128, 1024], F32, offset=R3_OFF + (56 + 4 * i) * KB) for i in range(2)]
    qstr = [AT(f"qstr{i}", [128, D], BF16, offset=R3_OFF + 4 * i * KB) for i in range(2)]
    sel_bf = AT("sel_bf", [16, D], BF16, offset=R3_OFF + 8 * KB)
    rb_sb = [AT(f"rb{i}", [128, 512], F32, offset=R3_OFF + (12 + 2 * i) * KB) for i in range(2)]
    xq_fm = AT("xq_fm", [128, C * S], BF16, offset=R3_OFF)

    Pt = nc.alloc_psum_tensor("Pt", [128, 4096], F32)
    pb = [Pt[:, i * 512:(i + 1) * 512] for i in range(8)]
    Ptb = Pt.bitcast(BF16)
    pbb = [Ptb[:, i * 1024: i * 1024 + 128] for i in range(8)]
    banks = Banks(em, pb)

    def pslice(b, n):
        return Pt[:, b * 512: b * 512 + n]

    _echo_cache = {}

    def echo_mm(nmm):
        if nmm in _echo_cache:
            return _echo_cache[nmm]
        n = em.op("scalar", lambda e: e.nop(), inc="ad", waits=[("mm", nmm)])
        _echo_cache[nmm] = n
        return n
    sem_names = ["lc", "lx0", "lx1", "lwn0", "lwn1", "sxq0", "sxq1",
                 "swq0", "swq1", "lq0", "lq1", "lo0", "lo1", "lf0", "lf1",
                 "lxt0", "lxt1", "lxq", "lxn", "sxn0", "sxn1", "lsel", "sxr", "lxr",
                 "sob0", "sob1",
                 "mm", "vd", "ad", "gc"]

    em.dma("gpsimd", ident[:], dident, "lc")
    em.dma("gpsimd", bq_t[:], dbq, "lc")
    em.dma("gpsimd", fb_t[:], dfb, "lc")
    em.dma("gpsimd", bk_r[:], dbk, "lc")
    em.dma("gpsimd", bv_r[:], dbv, "lc")
    em.dma("gpsimd", wkT[:], dwk, "lc")
    n_lc = em.dma("gpsimd", wvT[:], dwv, "lc")
    n_bq = n_lc
    em.wait("scalar", "lc", n_lc)
    em.wait("tensor", "lc", n_lc)
    em.op("vector", lambda e: e.memset(ones_rb[:], 1.0))
    em.op("vector", lambda e: e.memset(ones_cb[:], 1.0))
    em.op("vector", lambda e: e.memset(ones_rf[:], 1.0))
    em.op("vector", lambda e: e.memset(kv_sb[:], 0.0))
    n_kem0 = em.op("vector", lambda e: e.memset(kemb[:], 0.0), inc="vd")
    em.op("vector", lambda e: e.memset(cns0[:], 1e-6))
    em.op("vector", lambda e: e.memset(mgc[:], MAGIC))
    n_idb = em.op("vector", lambda e: e.tensor_copy(ident_bf[:], ident[:]),
                  inc="vd", waits=[("lc", n_lc)])
    em.wait("tensor", "vd", n_idb)
    n_cns = em.op("vector", lambda e: e.memset(cns1[:], -0.5), inc="vd")
    for i in range(16):
        em.op("vector", lambda e, i=i: e.tensor_copy(bqc[i][:], bq_t[:, i:i + 1]),
              waits=[("lc", n_bq)])
        em.op("vector", lambda e, i=i: e.tensor_copy(fbc[i][:], fb_t[:, i:i + 1]))
    n_bqc = em.op("vector", lambda e: e.tensor_copy(cns0[:], cns0[:]), inc="vd")
    n_ones = em.op("vector", lambda e: e.memset(st["t1"][:, 0:1], 0.0), inc="vd")
    n_bq1 = em.op("vector", lambda e: e.tensor_scalar_add(bq1_t[:], bq_t[:], 1.0),
                  inc="vd", waits=[("lc", n_bq)])

    xA_g = [None, None]
    xc_g = [None, None]
    qbf_g = [None, None]
    xntm_g = [None, None]
    rc_g = [None, None]
    rwc_g = [None, None]
    e_g = [None, None]
    rp_g = [None, None]
    hbuf_g = [None, None]
    qtmp_g = [None]
    qbfj_g = [None]
    wqst_g = [None] * 4

    sk_ready = [0] * NT
    v_ready = [0] * NT
    lxn_slab = [0] * TG
    xn_ev = [0] * NT

    def chunk_stats(t):
        sl = t % 2
        n_ld = em.dma("sync", xA[sl][:], dx[t * 128:(t + 1) * 128, :],
                      f"lx{sl}", waits=[xA_g[sl]])
        n_sum = em.op("scalar", lambda e, sl=sl, t=t: e.activation(
            xc[sl][:], xA[sl][:], Act.Identity, accum_out=st["sum"][:, t:t + 1]),
            inc="ad", waits=[(f"lx{sl}", n_ld), xc_g[sl]])
        em.wait("vector", "ad", n_sum)
        em.vop(lambda e, t=t: e.tensor_scalar_mul(
            st["mean"][:, t:t + 1], st["sum"][:, t:t + 1], INV2048))
        em.vop(lambda e, t=t: e.tensor_scalar_mul(
            st["mbias"][:, t:t + 1], st["mean"][:, t:t + 1], -1.0))
        n_mb = em.vop(lambda e, sl=sl, t=t: e.tensor_copy(
            mbc[sl][:], st["mbias"][:, t:t + 1]), waits=[xc_g[sl]])
        n_xc = em.op("scalar", lambda e, sl=sl, t=t: e.activation(
            xc[sl][:], xA[sl][:], Act.Identity, bias=mbc[sl][:],
            scale=1.0), inc="ad", waits=[(f"lx{sl}", n_ld), ("vd", n_mb), xc_g[sl]])
        n_sq = em.op("scalar", lambda e, sl=sl, t=t: e.activation(
            xA[sl][:], xc[sl][:], Act.Square, accum_out=st["ssq"][:, t:t + 1]),
            inc="ad")
        xA_g[sl] = [("ad", n_sq)]
        n_am = em.vop(lambda e, sl=sl, t=t: e.reduce_max(
            st["amax"][:, t:t + 1], xc[sl][:], axis=AxX,
            apply_absolute_value=True), waits=[("ad", n_xc)])
        return n_xc, n_am, 0, n_sq

    def stats_tail(pr, am_max, sq_max):
        c4 = slice(2 * pr, 2 * pr + 2)
        V = "vector"
        em.wait(V, "ad", sq_max)
        em.vop(lambda e: e.tensor_scalar_mul(st["ve"][:, c4], st["ssq"][:, c4], INV2048))
        n_ve = em.vop(lambda e: e.tensor_scalar_add(st["ve"][:, c4], st["ve"][:, c4], 1e-5))
        n_ln = em.op("scalar", lambda e: e.activation(
            st["t1"][:, c4], st["ve"][:, c4], Act.Ln), inc="ad", waits=[("vd", n_ve)])
        n_sd = em.op("scalar", lambda e: e.activation(
            st["r"][:, c4], st["t1"][:, c4], Act.Exp, scale=cns1[:]),
            inc="ad", waits=[("vd", n_cns)])
        em.wait(V, "ad", n_sd)
        for _ in range(2):
            em.vop(lambda e: e.tensor_mul(st["t1"][:, c4], st["r"][:, c4], st["r"][:, c4]))
            em.vop(lambda e: e.tensor_mul(st["t1"][:, c4], st["t1"][:, c4], st["ve"][:, c4]))
            em.vop(lambda e: e.tensor_scalar(st["t1"][:, c4], st["t1"][:, c4],
                                             -0.5, 1.5, Alu.mult, Alu.add))
            em.vop(lambda e: e.tensor_mul(st["r"][:, c4], st["r"][:, c4], st["t1"][:, c4]))
        em.wait(V, "vd", am_max)
        em.vop(lambda e: e.tensor_mul(st["t2"][:, c4], st["amax"][:, c4], st["r"][:, c4]))
        em.vop(lambda e: e.tensor_scalar(st["xs"][:, c4], st["t2"][:, c4], INV127,
                                         1e-12, Alu.mult, Alu.max))
        em.vop(lambda e: e.reciprocal(st["rxs"][:, c4], st["xs"][:, c4]))
        n = em.vop(lambda e: e.tensor_mul(st["s3"][:, c4], st["r"][:, c4],
                                          st["rxs"][:, c4]))
        return n

    def emit_slab_kv(g_slab):
        for t in range(g_slab * 4, g_slab * 4 + 4):
            for kind in ("k", "v"):
                bi = banks.take()
                em.wait("tensor", "ad", xn_ev[t])
                em.wait("tensor", "lc", n_lc)
                wT = wkT if kind == "k" else wvT
                for cc in range(C):
                    em.op("tensor", lambda e, bi=bi, cc=cc, t=t, wT=wT: e.matmul(
                        pb[bi][:], xn_fm[:, cc * S + t * 128: cc * S + t * 128 + 128],
                        wT[:, cc * KD:(cc + 1) * KD], start=(cc == 0), stop=False))
                em.wait("tensor", "vd", n_ones)
                br = bk_r if kind == "k" else bv_r
                em.op("tensor", lambda e, bi=bi, br=br: e.matmul(
                    pb[bi][:], ones_rb[:], br[:], start=False, stop=True),
                    inc="mm")
                nmm = em.cur("mm")
                if kind == "k":
                    p2 = t % 2
                    n_e = em.op("scalar", lambda e, bi=bi, p2=p2: e.activation(
                        e_bf[p2][:], pb[bi][:], Act.Exp), inc="ad",
                        waits=[("mm", nmm), e_g[p2]])
                    n_r = em.op("vector", lambda e, bi=bi, p2=p2: e.tensor_scalar(
                        rp_bf[p2][:], pb[bi][:], 1.0, 1.0, Alu.add, Alu.max),
                        inc="vd", waits=[("ad", n_e), rp_g[p2]])
                    banks.done(bi, "vd", n_r)
                    n_m = em.op("vector", lambda e, t=t, p2=p2: e.tensor_tensor(
                        sk_tm[:, t * KD:(t + 1) * KD], e_bf[p2][:], rp_bf[p2][:],
                        Alu.min), inc="vd", waits=[("vd", n_r)])
                    e_g[p2] = [("vd", n_m)]
                    rp_g[p2] = [("vd", n_m)]
                    sk_ready[t] = n_m
                else:
                    n_v = em.op("scalar", lambda e, bi=bi, t=t: e.activation(
                        v_tm[:, t * KD:(t + 1) * KD], pb[bi][:], Act.Identity),
                        inc="ad", waits=[("mm", nmm)])
                    banks.done(bi, "ad", n_v)
                    v_ready[t] = n_v

    def emit_wq(j):
        sl = j % 2
        n_lw = [0, 0]
        for h in range(2):
            n_lw[h] = em.dma("scalar", hbuf[h][:],
                             dwn[j * 128:(j + 1) * 128, h * 1024:(h + 1) * 1024],
                             f"lwn{h}", waits=[hbuf_g[h]])
        em.vop(lambda e, j=j: e.reduce_max(
            st["wmax"][:, j:j + 1], hbuf[0][:], axis=AxX,
            apply_absolute_value=True), waits=[("lwn0", n_lw[0])])
        em.vop(lambda e, j=j: e.reduce_max(
            st["wmaxb"][:, j:j + 1], hbuf[1][:], axis=AxX,
            apply_absolute_value=True), waits=[("lwn1", n_lw[1])])
        em.vop(lambda e, j=j: e.tensor_tensor(
            st["wmax"][:, j:j + 1], st["wmax"][:, j:j + 1],
            st["wmaxb"][:, j:j + 1], Alu.max))
        em.vop(lambda e, j=j: e.tensor_scalar(
            st["wsc"][:, j:j + 1], st["wmax"][:, j:j + 1], 1e-8, INV127,
            Alu.max, Alu.mult))
        em.vop(lambda e, j=j: e.tensor_copy(wscc[j][:], st["wsc"][:, j:j + 1]))
        em.vop(lambda e, j=j: e.reciprocal(
            st["rws"][:, j:j + 1], st["wsc"][:, j:j + 1]))
        n_rw = em.vop(lambda e, sl=sl, j=j: e.tensor_copy(
            rwc[sl][:], st["rws"][:, j:j + 1]), waits=[rwc_g[sl]])
        for h in range(2):
            n_sc = em.op("scalar", lambda e, h=h, sl=sl: e.activation(
                qtmp[:], hbuf[h][:], Act.Identity, scale=rwc[sl][:],
                bias=mgc[:]), inc="ad",
                waits=[("vd", n_rw), (f"lwn{h}", n_lw[h]), qtmp_g[0]])
            hbuf_g[h] = [("ad", n_sc)]
            n_rnd = em.op("vector", lambda e, h=h: e.tensor_scalar(
                qbfj[:, h * 1024:(h + 1) * 1024], qtmp[:], -MAGIC, None, Alu.add),
                inc="vd", waits=[("ad", n_sc), (qbfj_g[0] if h == 0 else None)])
            qtmp_g[0] = [("vd", n_rnd)]
        rwc_g[sl] = [("ad", em.cur("ad"))]
        n_rnd_all = em.cur("vd")
        last_mm = 0
        for cc in range(C):
            bi = banks.take()
            em.wait("tensor", "vd", n_rnd_all)
            nmm = em.op("tensor", lambda e, bi=bi, cc=cc: e.transpose(
                pbb[bi][:], qbfj[:, cc * 128:(cc + 1) * 128],
                ident_bf[:]), inc="mm")
            ri = cc % 4
            n_ev = em.op("scalar", lambda e, bi=bi, ri=ri: e.activation(
                wqT_st[ri][:], pbb[bi][:], Act.Identity), inc="ad",
                waits=[("mm", nmm), wqst_g[ri]])
            banks.done(bi, "ad", n_ev)
            n_st = em.dma("gpsimd",
                          d_wq2[(cc * 16 + j) * 128:(cc * 16 + j) * 128 + 128, :],
                          wqT_st[ri][:], f"swq{ri % 2}", waits=[("ad", n_ev)])
            wqst_g[ri] = [(f"swq{ri % 2}", n_st)]
            last_mm = nmm
        qbfj_g[0] = [("mm", last_mm)]

    n_stats_last = 0
    for pr in range(NT // 2):
        tg = pr // 2
        ts = (2 * pr, 2 * pr + 1)
        per = {t: chunk_stats(t) for t in ts}
        n_st = stats_tail(pr, max(per[t][1] for t in ts),
                          max(per[t][3] for t in ts))
        n_stats_last = n_st
        for t in ts:
            sl = t % 2
            n_rc = em.vop(lambda e, sl=sl, t=t: e.tensor_copy(
                rc[sl][:], st["r"][:, t:t + 1]), waits=[("vd", n_st), rc_g[sl]])
            n_xn = em.op("scalar", lambda e, sl=sl: e.activation(
                xntm[sl][:], xc[sl][:], Act.Identity, scale=rc[sl][:]),
                inc="ad", waits=[("vd", n_rc), xntm_g[sl]])
            rc_g[sl] = [("ad", n_xn)]
            last_tmm = 0
            for cc in range(C):
                bi = banks.take()
                em.wait("tensor", "ad", n_xn)
                em.wait("tensor", "vd", n_idb)
                last_tmm = em.op("tensor", lambda e, bi=bi, cc=cc, sl=sl: e.transpose(
                    pbb[bi][:], xntm[sl][:, cc * 128:(cc + 1) * 128],
                    ident_bf[:]), inc="mm")
                n_evx = em.op("scalar", lambda e, bi=bi, cc=cc, t=t: e.activation(
                    xn_fm[:, cc * S + t * 128: cc * S + t * 128 + 128],
                    pbb[bi][:], Act.Identity), inc="ad",
                    waits=[("mm", last_tmm)])
                banks.done(bi, "ad", n_evx)
            xn_ev[t] = em.cur("ad")
            xntm_g[sl] = [("mm", last_tmm)]
            n_q1 = em.op("vector", lambda e, sl=sl, t=t: e.tensor_scalar(
                xA[sl][:], xc[sl][:], st["s3"][:, t:t + 1], MAGIC,
                Alu.mult, Alu.add),
                inc="vd", waits=[("vd", n_st), xA_g[sl]])
            bsl = t % 2
            n_q2 = em.op("vector", lambda e, sl=sl, bsl=bsl: e.tensor_scalar(
                qbf[bsl][:], xA[sl][:], -MAGIC, None, Alu.add),
                inc="vd", waits=[("vd", n_q1), qbf_g[bsl]])
            xA_g[sl] = [("vd", n_q2)]
            n_sq_ = em.dma("gpsimd", d_xqsc[t * 128:(t + 1) * 128, :],
                           qbf[bsl][:], f"sxq{bsl}", waits=[("vd", n_q2)])
            qbf_g[bsl] = [(f"sxq{bsl}", n_sq_)]
            xc_g[sl] = [("vd", em.cur("vd")), ("ad", em.cur("ad"))]

        if pr % 2 == 1:
            emit_slab_kv(pr // 2)

    kvb = [banks.take() for _ in range(4)]
    ksb = [banks.take() for _ in range(4)]
    for b in kvb + ksb:
        banks.pin(b)
    for t in range(NT):
        em.wait("tensor", "vd", sk_ready[t])
        em.wait("tensor", "ad", v_ready[t])
        for gk in range(GK):
            em.op("tensor", lambda e, gk=gk, t=t: e.matmul(
                pslice(kvb[gk], 128),
                sk_tm[:, t * KD + gk * 128: t * KD + gk * 128 + 128],
                v_tm[:, t * KD + gk * 128: t * KD + gk * 128 + 128],
                start=(t == 0), stop=(t == NT - 1), skip_group_check=True))
            em.op("tensor", lambda e, gk=gk, t=t: e.matmul(
                pslice(ksb[gk], 1),
                sk_tm[:, t * KD + gk * 128: t * KD + gk * 128 + 128],
                ones_cb[:], start=(t == 0), stop=(t == NT - 1),
                skip_group_check=True),
                inc=("mm" if (t == NT - 1 and gk == GK - 1) else None))
    n_kvmm = em.cur("mm")
    n_kvec = echo_mm(n_kvmm)
    n_kv = [0] * GK
    for gk in range(GK):
        em.op("vector", lambda e, gk=gk: e.tensor_copy(
            kv_sb[:, gk * 132: gk * 132 + 128], pslice(kvb[gk], 128)),
            waits=[("ad", n_kvec)])
        n_kv[gk] = em.op("vector", lambda e, gk=gk: e.tensor_copy(
            kv_sb[:, gk * 132 + 128: gk * 132 + 129], pslice(ksb[gk], 1)),
            inc="vd", waits=[("ad", n_kvec)])
    for gk in range(GK):
        banks.unpin(kvb[gk], "vd", n_kv[gk])
        banks.unpin(ksb[gk], "vd", n_kv[gk])
    n_kemb = 0
    for h in range(NH):
        gk = h // 4
        n_kemb = em.op("vector", lambda e, h=h, gk=gk: e.tensor_copy(
            kemb[:, h * 16 + h: h * 16 + h + 1],
            kv_sb[:, gk * 132 + 128: gk * 132 + 129]), inc="vd",
            waits=[("vd", max(n_kv[gk], n_kem0))])

    n_A_ad = em.cur("ad")
    n_A_vd = em.cur("vd")
    n_A_st = [("sxq0", em.cur("sxq0")), ("sxq1", em.cur("sxq1"))]
    A_dead = [("ad", n_A_ad), ("vd", n_A_vd)] + n_A_st

    n_sel = em.dma("gpsimd", sel_bf[:], dsel, "lsel", waits=[A_dead])

    qstr_g = [A_dead, A_dead]
    hbuf_g[0] = A_dead
    hbuf_g[1] = A_dead
    qtmp_g[0] = A_dead
    qbfj_g[0] = A_dead
    for i in range(NH):
        sl = i % 2
        n_ld = em.dma("gpsimd", qstr[sl][:], dwq[i * 128:(i + 1) * 128, :],
                      f"lq{sl}", waits=[qstr_g[sl]])
        for g in range(TG):
            bi = banks.take()
            em.wait("tensor", f"lq{sl}", n_ld)
            em.wait("tensor", "ad", n_A_ad)
            for cc in range(C):
                em.op("tensor", lambda e, bi=bi, sl=sl, cc=cc, g=g: e.matmul(
                    pb[bi][:], qstr[sl][:, cc * 128:(cc + 1) * 128],
                    xn_fm[:, cc * S + g * 512: cc * S + g * 512 + 512],
                    start=(cc == 0), stop=(cc == C - 1)),
                    inc=("mm" if cc == C - 1 else None))
            nmm = em.cur("mm")
            p2 = (i * TG + g) % 2
            n_e = em.op("scalar", lambda e, bi=bi, i=i, p2=p2: e.activation(
                e_bf[p2][:], pb[bi][:], Act.Exp, bias=bqc[i][:]),
                inc="ad", waits=[("mm", nmm), ("vd", n_bqc), e_g[p2]])
            n_r = em.op("vector", lambda e, bi=bi, i=i, p2=p2: e.tensor_scalar(
                rp_bf[p2][:], pb[bi][:], bq1_t[:, i:i + 1], 1.0, Alu.add, Alu.max),
                inc="vd", waits=[("ad", n_e), ("vd", n_bq1), rp_g[p2]])
            banks.done(bi, "vd", n_r)
            n_m = em.op("vector", lambda e, i=i, g=g, p2=p2: e.tensor_tensor(
                sq_fm[:, i * S + g * 512: i * S + g * 512 + 512],
                e_bf[p2][:], rp_bf[p2][:], Alu.min), inc="vd",
                waits=[("vd", n_r)])
            e_g[p2] = [("vd", n_m)]
            rp_g[p2] = [("vd", n_m)]
        emit_wq(i)
        qstr_g[sl] = [("mm", em.cur("mm"))]
    n_qproj_mm = em.cur("mm")
    n_sq_done = em.cur("vd")
    wq_dead = [("vd", em.cur("vd")), ("ad", em.cur("ad")),
               ("mm", em.cur("mm"))]

    xq_ld_n = [0] * C

    def load_xq(cc, eng, extra):
        xq_ld_n[cc] = em.dma(
            eng, xq_fm[:, cc * S: cc * S + S],
            d_xqsc[:, cc * 128:(cc + 1) * 128], "lxq",
            waits=[A_dead] + list(extra), transpose=True)

    for cc in range(4, 12):
        load_xq(cc, "sync", [])
    for cc in range(12, 16):
        load_xq(cc, "sync",
                wq_dead + [("swq0", em.cur("swq0")), ("swq1", em.cur("swq1"))])

    rb_g = [A_dead, A_dead]
    attn_g = [0] * TG
    rec_guard = None
    for g in range(TG):
        if g == 1:
            load_xq(0, "sync", [("mm", n_qproj_mm)])
            load_xq(1, "sync", [("mm", n_qproj_mm)])
        bi_n = banks.take()
        em.wait("tensor", "vd", n_sq_done)
        em.wait("tensor", "vd", n_kemb)
        banks.pin(bi_n)
        for h in range(NH):
            em.op("tensor", lambda e, bi=bi_n, h=h, g=g: e.matmul(
                pb[bi][0:16, 0:512], kemb[:, h * 16:(h + 1) * 16],
                sq_fm[:, h * S + g * 512: h * S + g * 512 + 512],
                start=(h == 0), stop=(h == NH - 1)),
                inc=("mm" if h == NH - 1 else None))
        n_nmm = em.cur("mm")
        n_nec = echo_mm(n_nmm)
        n_re = em.op("vector", lambda e, bi=bi_n: e.tensor_scalar(
            pb[bi][32:48, 0:512], pb[bi][0:16, 0:512], 1e-6, None, Alu.add),
            inc="vd", waits=[("ad", n_nec)])
        n_rf = em.op("vector", lambda e, bi=bi_n: e.reciprocal(
            pb[bi][0:16, 0:512], pb[bi][32:48, 0:512]), inc="vd",
            waits=[("vd", n_re)])
        n_rb16 = em.op("vector", lambda e, bi=bi_n: e.tensor_copy(
            rec_bf[:], pb[bi][0:16, 0:512]), inc="vd",
            waits=[("vd", n_rf), rec_guard])
        banks.unpin(bi_n, "vd", n_rb16)
        for h in range(NH):
            p2 = h % 2
            bi_r = banks.take()
            em.wait("tensor", "vd", n_rb16)
            em.wait("tensor", "lsel", n_sel)
            n_rmm = em.op("tensor", lambda e, bi=bi_r, h=h: e.matmul(
                pb[bi][:], sel_bf[:, h * 128:(h + 1) * 128], rec_bf[:],
                start=True, stop=True), inc="mm")
            bi_x = banks.take()
            gk = h // 4
            em.wait("tensor", "vd", n_kv[gk])
            n_xmm = em.op("tensor", lambda e, bi=bi_x, h=h, g=g, gk=gk: e.matmul(
                pb[bi][:], kv_sb[:, gk * 132: gk * 132 + 128],
                sq_fm[:, h * S + g * 512: h * S + g * 512 + 512],
                start=True, stop=True), inc="mm")
            n_rbe = em.op("scalar", lambda e, bi=bi_r, p2=p2: e.activation(
                rb_sb[p2][:], pb[bi][:], Act.Identity), inc="ad",
                waits=[("mm", n_xmm), rb_g[p2]])
            banks.done(bi_r, "ad", n_rbe)
            n_div = em.op("vector", lambda e, bi=bi_x, h=h, g=g, p2=p2: e.tensor_tensor(
                attn_fm[:, h * S + g * 512: h * S + g * 512 + 512],
                pb[bi][:], rb_sb[p2][:], Alu.mult), inc="vd",
                waits=[("ad", n_rbe)])
            banks.done(bi_x, "vd", n_div)
            rb_g[p2] = [("vd", n_div)]
        rec_guard = [("ad", echo_mm(em.cur("mm")))]
        attn_g[g] = em.cur("vd")
    n_pd_mm = em.cur("mm")
    n_attn_vd = em.cur("vd")

    load_xq(2, "sync", [("mm", n_pd_mm)])
    load_xq(3, "sync", [("vd", n_attn_vd)])
    xq_all = em.cur("lxq")

    bi_t = banks.take()
    n_xst = em.op("tensor", lambda e, bi=bi_t: e.transpose(
        pb[bi][0:16, 0:128], st["xs"][:, 0:16], ident[:]), inc="mm",
        waits=[("vd", n_stats_last)])
    n_xsr = em.op("scalar", lambda e, bi=bi_t: e.activation(
        xsr_sb[:], pb[bi][0:16, 0:128], Act.Identity), inc="ad",
        waits=[("mm", max(n_xst, n_pd_mm))])
    banks.done(bi_t, "ad", n_xsr)
    n_xsst = em.dma("gpsimd", d_xsr.rearrange("a (b c) -> (a b) c", b=16),
                    xsr_sb[:], "sxr", waits=[("ad", n_xsr)])
    n_xrow = em.dma("gpsimd", xs_row[:], d_xsr, "lxr",
                    waits=[("sxr", n_xsst), ("mm", n_pd_mm)])
    xsb_ev = 0
    for g in range(TG):
        bi = banks.take()
        em.wait("tensor", "lxr", n_xrow)
        nmm = em.op("tensor", lambda e, bi=bi, g=g: e.matmul(
            pb[bi][:], ones_rf[:], xs_row[:, g * 512:(g + 1) * 512],
            start=True, stop=True), inc="mm")
        xsb_ev = em.op("scalar", lambda e, bi=bi, g=g: e.activation(
            xs_b[:, g * 512:(g + 1) * 512], pb[bi][:], Act.Identity),
            inc="ad", waits=[("mm", nmm)])
        banks.done(bi, "ad", xsb_ev)

    wostr_g = [None, None]
    fstr_g = [None, None]
    xt_g = [None, None]
    ost_g = [None, None]
    ystg_g = [None, None]
    ya_g = [None, None]
    yb_g = [None, None]
    ostage_g = [None, None]
    n_swq = [("swq0", em.cur("swq0")), ("swq1", em.cur("swq1"))]
    for i in range(C):
        sl = i % 2
        n_wol = em.dma("gpsimd", wostr[sl][:], dwo[i * 128:(i + 1) * 128, :],
                       f"lo{sl}", waits=[("mm", n_pd_mm), wostr_g[sl]])
        n_fl = 0
        for cc in range(C):
            n_fl = em.dma("sync", fstr[sl][:, cc * 128:(cc + 1) * 128],
                          d_wq2[(cc * 16 + i) * 128:(cc * 16 + i) * 128 + 128, :],
                          f"lf{sl}",
                          waits=[("mm", n_pd_mm)] + n_swq + [fstr_g[sl]])
        for g in range(TG):
            xsl = (i * TG + g) % 2
            n_xl = em.dma("scalar", xtile[xsl][:],
                          dxp[(i * TG + g) * 128:(i * TG + g) * 128 + 128, :],
                          f"lxt{xsl}", waits=[("mm", n_pd_mm), xt_g[xsl]])
            biA = banks.take()
            em.wait("tensor", f"lo{sl}", n_wol)
            em.wait("tensor", "vd", attn_g[g])
            for cc in range(C):
                em.op("tensor", lambda e, bi=biA, sl=sl, cc=cc, g=g: e.matmul(
                    pb[bi][:], wostr[sl][:, cc * 128:(cc + 1) * 128],
                    attn_fm[:, cc * S + g * 512: cc * S + g * 512 + 512],
                    start=(cc == 0), stop=(cc == C - 1)),
                    inc=("mm" if cc == C - 1 else None))
            nmmA = em.cur("mm")
            biB = banks.take()
            em.wait("tensor", f"lf{sl}", n_fl)
            em.wait("tensor", "lxq", xq_all)
            for cc in range(C):
                em.op("tensor", lambda e, bi=biB, sl=sl, cc=cc, g=g: e.matmul(
                    pb[bi][:], fstr[sl][:, cc * 128:(cc + 1) * 128],
                    xq_fm[:, cc * S + g * 512: cc * S + g * 512 + 512],
                    start=(cc == 0), stop=(cc == C - 1)),
                    inc=("mm" if cc == C - 1 else None))
            nmmB = em.cur("mm")
            osl = (i * TG + g) % 2
            n_evA = em.op("scalar", lambda e, bi=biA, osl=osl, i=i: e.activation(
                ostg0[osl][:], pb[bi][:], Act.Identity, bias=fbc[i][:]),
                inc="ad", waits=[("mm", nmmA), ost_g[osl]])
            banks.done(biA, "ad", n_evA)
            n_evB = em.op("scalar", lambda e, bi=biB, osl=osl, i=i: e.activation(
                ystg0[osl][:], pb[bi][:], Act.Identity, scale=wscc[i][:]),
                inc="ad", waits=[("mm", nmmB), ystg_g[osl]])
            banks.done(biB, "ad", n_evB)
            n_y1 = em.op("vector", lambda e, osl=osl, g=g: e.tensor_tensor(
                ysa[osl][:], ystg0[osl][:], xs_b[:, g * 512:(g + 1) * 512],
                Alu.mult), inc="vd",
                waits=[("ad", n_evB), ("ad", xsb_ev), ya_g[osl]])
            ystg_g[osl] = [("vd", n_y1)]
            n_y2 = em.op("vector", lambda e, osl=osl: e.tensor_tensor(
                ysb[osl][:], ysa[osl][:], ostg0[osl][:], Alu.add), inc="vd",
                waits=[("ad", n_evA), yb_g[osl]])
            ya_g[osl] = [("vd", n_y2)]
            ost_g[osl] = [("vd", n_y2)]
            n_y3 = em.op("vector", lambda e, osl=osl, xsl=xsl: e.tensor_tensor(
                ostage[osl][:], ysb[osl][:], xtile[xsl][:], Alu.add), inc="vd",
                waits=[(f"lxt{xsl}", n_xl), ostage_g[osl]])
            yb_g[osl] = [("vd", n_y3)]
            xt_g[xsl] = [("vd", n_y3)]
            n_os = em.dma("gpsimd",
                          dout[i * 128:(i + 1) * 128, g * 512:(g + 1) * 512],
                          ostage[osl][:], f"sob{osl}", waits=[("vd", n_y3)])
            ostage_g[osl] = [(f"sob{osl}", n_os)]
        wostr_g[sl] = [("mm", em.cur("mm"))]
        fstr_g[sl] = [("mm", em.cur("mm"))]

    return _finish(nc, em, sem_names)


def _finish(nc, em, sem_names):
    for sname in ("sob0", "sob1"):
        em.wait("sync", sname, em.cur(sname))
        em.wait("gpsimd", sname, em.cur(sname))
    for sname in ("sxq0", "sxq1", "swq0", "swq1", "sxr"):
        em.wait("gpsimd", sname, em.cur(sname))
    em.wait("vector", "mm", em.cur("mm"))
    em.wait("scalar", "mm", em.cur("mm"))

    for s in sem_names:
        em.sems[s] = nc.alloc_semaphore(s)
    with nc.Block() as block:

        @block.sync
        def _(e):
            for f in em.scripts["sync"]:
                f(e)

        @block.tensor
        def _(e):
            for f in em.scripts["tensor"]:
                f(e)

        @block.vector
        def _(e):
            for f in em.scripts["vector"]:
                f(e)

        @block.scalar
        def _(e):
            for f in em.scripts["scalar"]:
                f(e)

        @block.gpsimd
        def _(e):
            for f in em.scripts["gpsimd"]:
                f(e)

    return nc


def make_inputs(b, x, Wq, bq, Wk, bk, Wv, bv, Wo, Wf, fb):
    f32 = np.float32
    xb = np.ascontiguousarray(x[b], dtype=f32)
    x_pk = np.ascontiguousarray(
        xb.T.reshape(C, 128, TG, 512).transpose(0, 2, 1, 3)).reshape(C * TG * 128, 512)
    wq_pk = np.ascontiguousarray(
        Wq.reshape(NH, 128, C, 128).transpose(0, 3, 2, 1)).reshape(NH * 128, D)
    wo_pk = np.ascontiguousarray(
        Wo.reshape(C, 128, C, 128).transpose(0, 3, 2, 1)).reshape(C * 128, D)
    wk_pk = np.ascontiguousarray(
        Wk.T.reshape(C, 128, KD).transpose(1, 0, 2)).reshape(128, C * KD)
    wv_pk = np.ascontiguousarray(
        Wv.T.reshape(C, 128, KD).transpose(1, 0, 2)).reshape(128, C * KD)
    sel = np.zeros((16, D), f32)
    for h in range(16):
        sel[h, h * 128:(h + 1) * 128] = 1.0
    return {
        "x_nat": xb,
        "x_pk": x_pk,
        "wq_pk": wq_pk,
        "wo_pk": wo_pk,
        "wk_pk": wk_pk,
        "wv_pk": wv_pk,
        "wn_nat": np.ascontiguousarray(Wf, dtype=f32),
        "bq_t": np.ascontiguousarray(bq.reshape(16, 128).T),
        "fb_t": np.ascontiguousarray(fb.reshape(16, 128).T),
        "bk_r": np.ascontiguousarray(bk[None, :]),
        "bv_r": np.ascontiguousarray(bv[None, :]),
        "sel_h": sel,
        "ident": np.eye(128, dtype=f32),
    }


def kernel(L0_attn_hidden_states, L0_attn_Wq, L0_attn_bq, L0_attn_Wk,
           L0_attn_bk, L0_attn_Wv, L0_attn_bv, L0_attn_Wo, L0_ffn_weight,
           L0_ffn_bias):
    x = np.asarray(L0_attn_hidden_states, dtype=np.float32)
    Wq = np.asarray(L0_attn_Wq, dtype=np.float32)
    bq = np.asarray(L0_attn_bq, dtype=np.float32)
    Wk = np.asarray(L0_attn_Wk, dtype=np.float32)
    bk = np.asarray(L0_attn_bk, dtype=np.float32)
    Wv = np.asarray(L0_attn_Wv, dtype=np.float32)
    bv = np.asarray(L0_attn_bv, dtype=np.float32)
    Wo = np.asarray(L0_attn_Wo, dtype=np.float32)
    Wf = np.asarray(L0_ffn_weight, dtype=np.float32)
    fb = np.asarray(L0_ffn_bias, dtype=np.float32)
    nc = build_nc()
    in_maps = [make_inputs(b, x, Wq, bq, Wk, bk, Wv, bv, Wo, Wf, fb)
               for b in range(8)]
    res = run_bass_kernel_spmd(nc, in_maps, core_ids=list(range(8)))
    out = np.stack([np.ascontiguousarray(r["out"].T) for r in res.results])
    return out


# version 15 (final)
# speedup vs baseline: 1.2259x; 1.1870x over previous
import numpy as np
import concourse.bass as bass
import concourse.mybir as mybir
from concourse.bass_utils import run_bass_kernel_spmd

F32 = mybir.dt.float32
BF16 = mybir.dt.bfloat16
I32 = mybir.dt.int32
Alu = mybir.AluOpType
Act = mybir.ActivationFunctionType
AxX = mybir.AxisListType.X

S = 2048
D = 2048
C = 16
NT = 16
TG = 4
NH = 16
GK = 4
KD = 512

R1_OFF = 32704
R2_OFF = 98240
R3_OFF = 163776
KB = 1024

INV2048 = float(np.float32(1.0 / 2048.0))
INV127 = float(np.float32(1.0 / 127.0))
MAGIC = 12582912.0


class Emit:
    ENGS = ("sync", "tensor", "vector", "scalar", "gpsimd")

    def __init__(self):
        self.scripts = {e: [] for e in self.ENGS}
        self.ctr = {}
        self.waited = {e: {} for e in self.ENGS}
        self.sems = {}

    def inc_val(self, sem, n):
        self.ctr[sem] = self.ctr.get(sem, 0) + n
        return self.ctr[sem]

    def cur(self, sem):
        return self.ctr.get(sem, 0)

    def wait(self, eng, sem, val):
        if val is None or val <= 0:
            return
        if self.waited[eng].get(sem, 0) >= val:
            return
        self.waited[eng][sem] = val
        self.scripts[eng].append(
            lambda e, s=sem, v=val: e.wait_ge(self.sems[s], v))

    def waits(self, eng, ws):
        for w in ws:
            if w is None:
                continue
            if isinstance(w, list):
                self.waits(eng, w)
            else:
                self.wait(eng, w[0], w[1])

    def op(self, eng, fn, inc=None, waits=()):
        self.waits(eng, waits)
        if inc is None:
            self.scripts[eng].append(fn)
            return None
        n = self.inc_val(inc, 1)
        self.scripts[eng].append(
            lambda e, f=fn, s=inc: f(e).then_inc(self.sems[s], 1))
        return n

    def vop(self, fn, waits=()):
        n = self.op("vector", fn, inc="vd", waits=waits)
        self.wait("vector", "vd", n)
        return n

    def dma(self, eng, out, in_, sem, waits=(), **kw):
        self.waits(eng, waits)
        n = self.inc_val(sem, 16)
        self.scripts[eng].append(
            lambda e, o=out, i=in_, s=sem, kw=kw: e.dma_start(out=o, in_=i, **kw)
            .then_inc(self.sems[s], 16))
        return n


class Banks:
    def __init__(self, em, pb):
        self.em = em
        self.pb = pb
        self.last = [None] * len(pb)
        self.rr = 0
        self.pinned = set()

    def take(self, n=1):
        nb = len(self.pb)
        for _ in range(nb):
            s = self.rr % nb
            self.rr += 1
            if s + n > nb:
                continue
            if any((s + k) in self.pinned for k in range(n)):
                continue
            for k in range(n):
                if self.last[s + k] is not None:
                    self.em.wait("tensor", self.last[s + k][0], self.last[s + k][1])
            self.rr = s + n
            return s
        raise RuntimeError("no free psum bank group")

    def done(self, i, sem, cnt):
        self.last[i] = (sem, cnt)

    def pin(self, i):
        self.pinned.add(i)
        if self.last[i] is not None:
            self.em.wait("tensor", self.last[i][0], self.last[i][1])

    def unpin(self, i, sem, cnt):
        self.pinned.discard(i)
        self.last[i] = (sem, cnt)


def build_nc():
    nc = bass.Bass("TRN2", target_bir_lowering=False,
                   detect_race_conditions=False)
    em = Emit()

    dx = nc.dram_tensor("x_nat", [S, D], F32, kind="ExternalInput").ap()
    dxp = nc.dram_tensor("x_pk", [C * TG * 128, 512], F32, kind="ExternalInput").ap()
    dwq = nc.dram_tensor("wq_pk", [NH * 128, D], F32, kind="ExternalInput").ap()
    dwo = nc.dram_tensor("wo_pk", [C * 128, D], F32, kind="ExternalInput").ap()
    dwk = nc.dram_tensor("wk_pk", [128, C * KD], F32, kind="ExternalInput").ap()
    dwv = nc.dram_tensor("wv_pk", [128, C * KD], F32, kind="ExternalInput").ap()
    dwn = nc.dram_tensor("wn_nat", [D, D], F32, kind="ExternalInput").ap()
    dbq = nc.dram_tensor("bq_t", [128, 16], F32, kind="ExternalInput").ap()
    dfb = nc.dram_tensor("fb_t", [128, 16], F32, kind="ExternalInput").ap()
    dbk = nc.dram_tensor("bk_r", [1, KD], F32, kind="ExternalInput").ap()
    dbv = nc.dram_tensor("bv_r", [1, KD], F32, kind="ExternalInput").ap()
    dsel = nc.dram_tensor("sel_h", [16, D], F32, kind="ExternalInput").ap()
    dident = nc.dram_tensor("ident", [128, 128], F32, kind="ExternalInput").ap()
    dout = nc.dram_tensor("out", [D, S], F32, kind="ExternalOutput").ap()
    d_wq2 = nc.dram_tensor("wq_sc", [C * C * 128, 128], BF16, kind="Internal").ap()
    d_xqsc = nc.dram_tensor("xq_sc", [S, D], BF16, kind="Internal").ap()
    d_xnsc = nc.dram_tensor("xn_sc", [S, D], BF16, kind="Internal").ap()
    d_xsr = nc.dram_tensor("xsr_sc", [1, S], F32, kind="Internal").ap()

    A = nc.alloc_sbuf_tensor
    ident = A("sb_ident", [128, 128], F32)
    ident_bf = A("sb_identb", [128, 128], BF16)
    ones_rb = A("ones_rb", [1, 128], BF16)
    ones_cb = A("ones_cb", [128, 1], BF16)
    ones_rf = A("ones_rf", [1, 128], F32)
    stn = ["sum", "mean", "mbias", "ssq", "ve", "r", "amax", "xs", "rxs",
           "s3", "t1", "t2", "wmax", "wmaxb", "wsc", "rws"]
    st = {k: A("st_" + k, [128, 16], F32) for k in stn}
    bq_t = A("sb_bq", [128, 16], F32)
    bq1_t = A("bq1_t", [128, 16], F32)
    fb_t = A("sb_fb", [128, 16], F32)
    bk_r = A("sb_bk", [1, KD], BF16)
    bv_r = A("sb_bv", [1, KD], BF16)
    kv_sb = A("kv_sb", [128, 4 * 132], BF16)
    kemb = A("kemb", [128, 256], BF16)
    e_bf = [A(f"e_bf{i}", [128, 512], BF16) for i in range(2)]
    rp_bf = [A(f"rp_bf{i}", [128, 512], BF16) for i in range(2)]
    cns0 = A("cns0", [128, 1], F32)
    cns1 = A("cns1", [128, 1], F32)
    mgc = A("mgc", [128, 1], F32)
    bqc = [A(f"bqc{i}", [128, 1], F32) for i in range(16)]
    fbc = [A(f"fbc{i}", [128, 1], F32) for i in range(16)]
    wscc = [A(f"wscc{i}", [128, 1], F32) for i in range(16)]
    mbc = [A(f"mbc{i}", [128, 1], F32) for i in range(2)]
    rc = [A(f"rc{i}", [128, 1], F32) for i in range(2)]
    rwc = [A(f"rwc{i}", [128, 1], F32) for i in range(2)]
    wqT_st = [A(f"wqT{i}", [128, 128], BF16) for i in range(4)]
    rec_bf = A("rec_bf", [16, 512], BF16)
    assert nc.sbuf_base <= R1_OFF, f"persist overflow: {nc.sbuf_base}"

    AT = nc.alloc_sbuf_tensor_at
    xn_fm = AT("xn_fm", [128, C * S], BF16, offset=R1_OFF)
    attn_fm = AT("attn_fm", [128, C * S], BF16, offset=R1_OFF)
    wkT = AT("wkT", [128, C * KD], BF16, offset=R2_OFF)
    wvT = AT("wvT", [128, C * KD], BF16, offset=R2_OFF + 16 * KB)
    sk_tm = AT("sk_tm", [128, NT * KD], BF16, offset=R2_OFF + 32 * KB)
    v_tm = AT("v_tm", [128, NT * KD], BF16, offset=R2_OFF + 48 * KB)
    sq_fm = AT("sq_fm", [128, NH * S], BF16, offset=R2_OFF)
    ostage = [AT(f"ostage{i}", [128, 512], F32, offset=R2_OFF + 2 * i * KB) for i in range(2)]
    xtile = [AT(f"xtile{i}", [128, 512], F32, offset=R2_OFF + (4 + 2 * i) * KB) for i in range(2)]
    wostr = [AT(f"wostr{i}", [128, D], BF16, offset=R2_OFF + (8 + 4 * i) * KB) for i in range(2)]
    xs_row = AT("xs_row", [1, S], F32, offset=R2_OFF + 16 * KB)
    xs_b = AT("xs_b", [128, S], F32, offset=R2_OFF + 24 * KB)
    fstr = [AT(f"fstr{i}", [128, D], BF16, offset=R2_OFF + (32 + 4 * i) * KB) for i in range(2)]
    ysa = [AT(f"ysa{i}", [128, 512], F32, offset=R2_OFF + (40 + 2 * i) * KB) for i in range(2)]
    ysb = [AT(f"ysb{i}", [128, 512], F32, offset=R2_OFF + (44 + 2 * i) * KB) for i in range(2)]
    xsr_sb = AT("xsr_sb", [16, 128], F32, offset=R2_OFF + 48 * KB)
    ostg0 = [AT(f"ostg0_{i}", [128, 512], F32, offset=R2_OFF + (49 + 2 * i) * KB) for i in range(2)]
    ystg0 = [AT(f"ystg0_{i}", [128, 512], F32, offset=R2_OFF + (53 + 2 * i) * KB) for i in range(2)]
    xA = [AT(f"xA{i}", [128, D], F32, offset=R3_OFF + 8 * i * KB) for i in range(2)]
    xc = [AT(f"xc{i}", [128, D], F32, offset=R3_OFF + (16 + 8 * i) * KB) for i in range(2)]
    xntm = [AT(f"xntm{i}", [128, D], BF16, offset=R3_OFF + (32 + 4 * i) * KB) for i in range(2)]
    qbf = [AT(f"qbf{i}", [128, D], BF16, offset=R3_OFF + (40 + 4 * i) * KB) for i in range(2)]
    qtmp = AT("qtmp", [128, 1024], F32, offset=R3_OFF + 48 * KB)
    qbfj = AT("qbfj", [128, D], BF16, offset=R3_OFF + 52 * KB)
    hbuf = [AT(f"hbuf{i}", [128, 1024], F32, offset=R3_OFF + (56 + 4 * i) * KB) for i in range(2)]
    qstr = [AT(f"qstr{i}", [128, D], BF16, offset=R3_OFF + 4 * i * KB) for i in range(2)]
    sel_bf = AT("sel_bf", [16, D], BF16, offset=R3_OFF + 8 * KB)
    rb_sb = [AT(f"rb{i}", [128, 512], F32, offset=R3_OFF + (12 + 2 * i) * KB) for i in range(2)]
    xq_fm = AT("xq_fm", [128, C * S], BF16, offset=R3_OFF)

    Pt = nc.alloc_psum_tensor("Pt", [128, 4096], F32)
    pb = [Pt[:, i * 512:(i + 1) * 512] for i in range(8)]
    Ptb = Pt.bitcast(BF16)
    pbb = [Ptb[:, i * 1024: i * 1024 + 128] for i in range(8)]
    banks = Banks(em, pb)

    def pslice(b, n):
        return Pt[:, b * 512: b * 512 + n]

    _echo_cache = {}

    def echo_mm(nmm):
        if nmm in _echo_cache:
            return _echo_cache[nmm]
        n = em.op("scalar", lambda e: e.nop(), inc="ad", waits=[("mm", nmm)])
        _echo_cache[nmm] = n
        return n
    sem_names = ["lc", "lx0", "lx1", "lwn0", "lwn1", "sxq0", "sxq1",
                 "swq0", "swq1", "lq0", "lq1", "lo0", "lo1", "lf0", "lf1",
                 "lxt0", "lxt1", "lxq", "lxn", "sxn0", "sxn1", "lsel", "sxr", "lxr",
                 "sob0", "sob1",
                 "mm", "vd", "ad", "gc"]

    em.dma("gpsimd", ident[:], dident, "lc")
    em.dma("gpsimd", bq_t[:], dbq, "lc")
    em.dma("gpsimd", fb_t[:], dfb, "lc")
    em.dma("gpsimd", bk_r[:], dbk, "lc")
    em.dma("gpsimd", bv_r[:], dbv, "lc")
    em.dma("gpsimd", wkT[:], dwk, "lc")
    n_lc = em.dma("gpsimd", wvT[:], dwv, "lc")
    n_bq = n_lc
    em.wait("scalar", "lc", n_lc)
    em.wait("tensor", "lc", n_lc)
    em.op("vector", lambda e: e.memset(ones_rb[:], 1.0))
    em.op("vector", lambda e: e.memset(ones_cb[:], 1.0))
    em.op("vector", lambda e: e.memset(ones_rf[:], 1.0))
    em.op("vector", lambda e: e.memset(kv_sb[:], 0.0))
    n_kem0 = em.op("vector", lambda e: e.memset(kemb[:], 0.0), inc="vd")
    em.op("vector", lambda e: e.memset(cns0[:], 1e-6))
    em.op("vector", lambda e: e.memset(mgc[:], MAGIC))
    n_idb = em.op("vector", lambda e: e.tensor_copy(ident_bf[:], ident[:]),
                  inc="vd", waits=[("lc", n_lc)])
    em.wait("tensor", "vd", n_idb)
    n_cns = em.op("vector", lambda e: e.memset(cns1[:], -0.5), inc="vd")
    for i in range(16):
        em.op("vector", lambda e, i=i: e.tensor_copy(bqc[i][:], bq_t[:, i:i + 1]),
              waits=[("lc", n_bq)])
        em.op("vector", lambda e, i=i: e.tensor_copy(fbc[i][:], fb_t[:, i:i + 1]))
    n_bqc = em.op("vector", lambda e: e.tensor_copy(cns0[:], cns0[:]), inc="vd")
    n_ones = em.op("vector", lambda e: e.memset(st["t1"][:, 0:1], 0.0), inc="vd")
    n_bq1 = em.op("vector", lambda e: e.tensor_scalar_add(bq1_t[:], bq_t[:], 1.0),
                  inc="vd", waits=[("lc", n_bq)])

    xA_g = [None, None]
    xc_g = [None, None]
    qbf_g = [None, None]
    xntm_g = [None, None]
    rc_g = [None, None]
    rwc_g = [None, None]
    e_g = [None, None]
    rp_g = [None, None]
    hbuf_g = [None, None]
    qtmp_g = [None]
    qbfj_g = [None]
    wqst_g = [None] * 4
    wq_rnd = [0]

    sk_ready = [0] * NT
    v_ready = [0] * NT
    lxn_slab = [0] * TG
    xn_ev = [0] * NT
    xn_ev_vd = [0] * NT

    def chunk_stats(t):
        sl = t % 2
        n_ld = em.dma("sync", xA[sl][:], dx[t * 128:(t + 1) * 128, :],
                      f"lx{sl}", waits=[xA_g[sl]])
        n_sum = em.op("scalar", lambda e, sl=sl, t=t: e.activation(
            xc[sl][:], xA[sl][:], Act.Identity, accum_out=st["sum"][:, t:t + 1]),
            inc="ad", waits=[(f"lx{sl}", n_ld), xc_g[sl]])
        em.wait("vector", "ad", n_sum)
        em.vop(lambda e, t=t: e.tensor_scalar_mul(
            st["mean"][:, t:t + 1], st["sum"][:, t:t + 1], INV2048))
        em.vop(lambda e, t=t: e.tensor_scalar_mul(
            st["mbias"][:, t:t + 1], st["mean"][:, t:t + 1], -1.0))
        n_mb = em.vop(lambda e, sl=sl, t=t: e.tensor_copy(
            mbc[sl][:], st["mbias"][:, t:t + 1]), waits=[xc_g[sl]])
        n_xc = em.op("scalar", lambda e, sl=sl, t=t: e.activation(
            xc[sl][:], xA[sl][:], Act.Identity, bias=mbc[sl][:],
            scale=1.0), inc="ad", waits=[(f"lx{sl}", n_ld), ("vd", n_mb), xc_g[sl]])
        n_sq = em.op("scalar", lambda e, sl=sl, t=t: e.activation(
            xA[sl][:], xc[sl][:], Act.Square, accum_out=st["ssq"][:, t:t + 1]),
            inc="ad")
        xA_g[sl] = [("ad", n_sq)]
        n_am = em.vop(lambda e, sl=sl, t=t: e.reduce_max(
            st["amax"][:, t:t + 1], xc[sl][:], axis=AxX,
            apply_absolute_value=True), waits=[("ad", n_xc)])
        return n_xc, n_am, 0, n_sq

    def stats_tail(pr, am_max, sq_max):
        c4 = slice(2 * pr, 2 * pr + 2)
        V = "vector"
        em.wait(V, "ad", sq_max)
        em.vop(lambda e: e.tensor_scalar_mul(st["ve"][:, c4], st["ssq"][:, c4], INV2048))
        n_ve = em.vop(lambda e: e.tensor_scalar_add(st["ve"][:, c4], st["ve"][:, c4], 1e-5))
        n_ln = em.op("scalar", lambda e: e.activation(
            st["t1"][:, c4], st["ve"][:, c4], Act.Ln), inc="ad", waits=[("vd", n_ve)])
        n_sd = em.op("scalar", lambda e: e.activation(
            st["r"][:, c4], st["t1"][:, c4], Act.Exp, scale=cns1[:]),
            inc="ad", waits=[("vd", n_cns)])
        em.wait(V, "ad", n_sd)
        for _ in range(2):
            em.vop(lambda e: e.tensor_mul(st["t1"][:, c4], st["r"][:, c4], st["r"][:, c4]))
            em.vop(lambda e: e.tensor_mul(st["t1"][:, c4], st["t1"][:, c4], st["ve"][:, c4]))
            em.vop(lambda e: e.tensor_scalar(st["t1"][:, c4], st["t1"][:, c4],
                                             -0.5, 1.5, Alu.mult, Alu.add))
            em.vop(lambda e: e.tensor_mul(st["r"][:, c4], st["r"][:, c4], st["t1"][:, c4]))
        em.wait(V, "vd", am_max)
        em.vop(lambda e: e.tensor_mul(st["t2"][:, c4], st["amax"][:, c4], st["r"][:, c4]))
        em.vop(lambda e: e.tensor_scalar(st["xs"][:, c4], st["t2"][:, c4], INV127,
                                         1e-12, Alu.mult, Alu.max))
        em.vop(lambda e: e.reciprocal(st["rxs"][:, c4], st["xs"][:, c4]))
        n = em.vop(lambda e: e.tensor_mul(st["s3"][:, c4], st["r"][:, c4],
                                          st["rxs"][:, c4]))
        return n

    def emit_slab_kv(g_slab):
        for t in range(g_slab * 4, g_slab * 4 + 4):
            for kind in ("k", "v"):
                bi = banks.take()
                em.wait("tensor", "vd", xn_ev_vd[t])
                em.wait("tensor", "lc", n_lc)
                wT = wkT if kind == "k" else wvT
                for cc in range(C):
                    em.op("tensor", lambda e, bi=bi, cc=cc, t=t, wT=wT: e.matmul(
                        pb[bi][:], xn_fm[:, cc * S + t * 128: cc * S + t * 128 + 128],
                        wT[:, cc * KD:(cc + 1) * KD], start=(cc == 0), stop=False))
                em.wait("tensor", "vd", n_ones)
                br = bk_r if kind == "k" else bv_r
                em.op("tensor", lambda e, bi=bi, br=br: e.matmul(
                    pb[bi][:], ones_rb[:], br[:], start=False, stop=True),
                    inc="mm")
                nmm = em.cur("mm")
                if kind == "k":
                    p2 = t % 2
                    n_e = em.op("scalar", lambda e, bi=bi, p2=p2: e.activation(
                        e_bf[p2][:], pb[bi][:], Act.Exp), inc="ad",
                        waits=[("mm", nmm), e_g[p2]])
                    n_r = em.op("vector", lambda e, bi=bi, p2=p2: e.tensor_scalar(
                        rp_bf[p2][:], pb[bi][:], 1.0, 1.0, Alu.add, Alu.max),
                        inc="vd", waits=[("ad", n_e), rp_g[p2]])
                    banks.done(bi, "vd", n_r)
                    n_m = em.op("vector", lambda e, t=t, p2=p2: e.tensor_tensor(
                        sk_tm[:, t * KD:(t + 1) * KD], e_bf[p2][:], rp_bf[p2][:],
                        Alu.min), inc="vd", waits=[("vd", n_r)])
                    e_g[p2] = [("vd", n_m)]
                    rp_g[p2] = [("vd", n_m)]
                    sk_ready[t] = n_m
                else:
                    n_v = em.op("scalar", lambda e, bi=bi, t=t: e.activation(
                        v_tm[:, t * KD:(t + 1) * KD], pb[bi][:], Act.Identity),
                        inc="ad", waits=[("mm", nmm)])
                    banks.done(bi, "ad", n_v)
                    v_ready[t] = n_v

    def emit_wq(j):
        sl = j % 2
        n_lw = [0, 0]
        for h in range(2):
            n_lw[h] = em.dma("scalar", hbuf[h][:],
                             dwn[j * 128:(j + 1) * 128, h * 1024:(h + 1) * 1024],
                             f"lwn{h}", waits=[hbuf_g[h]])
        em.vop(lambda e, j=j: e.reduce_max(
            st["wmax"][:, j:j + 1], hbuf[0][:], axis=AxX,
            apply_absolute_value=True), waits=[("lwn0", n_lw[0])])
        em.vop(lambda e, j=j: e.reduce_max(
            st["wmaxb"][:, j:j + 1], hbuf[1][:], axis=AxX,
            apply_absolute_value=True), waits=[("lwn1", n_lw[1])])
        em.vop(lambda e, j=j: e.tensor_tensor(
            st["wmax"][:, j:j + 1], st["wmax"][:, j:j + 1],
            st["wmaxb"][:, j:j + 1], Alu.max))
        em.vop(lambda e, j=j: e.tensor_scalar(
            st["wsc"][:, j:j + 1], st["wmax"][:, j:j + 1], 1e-8, INV127,
            Alu.max, Alu.mult))
        em.vop(lambda e, j=j: e.tensor_copy(wscc[j][:], st["wsc"][:, j:j + 1]))
        em.vop(lambda e, j=j: e.reciprocal(
            st["rws"][:, j:j + 1], st["wsc"][:, j:j + 1]))
        n_rw = em.vop(lambda e, sl=sl, j=j: e.tensor_copy(
            rwc[sl][:], st["rws"][:, j:j + 1]), waits=[rwc_g[sl]])
        for h in range(2):
            n_sc = em.op("scalar", lambda e, h=h, sl=sl: e.activation(
                qtmp[:], hbuf[h][:], Act.Identity, scale=rwc[sl][:],
                bias=mgc[:]), inc="ad",
                waits=[("vd", n_rw), (f"lwn{h}", n_lw[h]), qtmp_g[0]])
            hbuf_g[h] = [("ad", n_sc)]
            n_rnd = em.op("vector", lambda e, h=h: e.tensor_scalar(
                qbfj[:, h * 1024:(h + 1) * 1024], qtmp[:], -MAGIC, None, Alu.add),
                inc="vd", waits=[("ad", n_sc), (qbfj_g[0] if h == 0 else None)])
            qtmp_g[0] = [("vd", n_rnd)]
        rwc_g[sl] = [("ad", em.cur("ad"))]
        wq_rnd[0] = em.cur("vd")

    def emit_wq_back(j):
        last_mm = 0
        for cc in range(C):
            bi = banks.take()
            em.wait("tensor", "vd", wq_rnd[0])
            nmm = em.op("tensor", lambda e, bi=bi, cc=cc: e.transpose(
                pbb[bi][:], qbfj[:, cc * 128:(cc + 1) * 128],
                ident_bf[:]), inc="mm")
            ri = cc % 4
            n_ev = em.op("scalar", lambda e, bi=bi, ri=ri: e.activation(
                wqT_st[ri][:], pbb[bi][:], Act.Identity), inc="ad",
                waits=[("mm", nmm), wqst_g[ri]])
            banks.done(bi, "ad", n_ev)
            n_st = em.dma("gpsimd",
                          d_wq2[(cc * 16 + j) * 128:(cc * 16 + j) * 128 + 128, :],
                          wqT_st[ri][:], f"swq{ri % 2}", waits=[("ad", n_ev)])
            wqst_g[ri] = [(f"swq{ri % 2}", n_st)]
            last_mm = nmm
        qbfj_g[0] = [("mm", last_mm)]

    n_stats_last = 0
    for pr in range(NT // 2):
        tg = pr // 2
        ts = (2 * pr, 2 * pr + 1)
        per = {t: chunk_stats(t) for t in ts}
        n_st = stats_tail(pr, max(per[t][1] for t in ts),
                          max(per[t][3] for t in ts))
        n_stats_last = n_st
        for t in ts:
            sl = t % 2
            n_rc = em.vop(lambda e, sl=sl, t=t: e.tensor_copy(
                rc[sl][:], st["r"][:, t:t + 1]), waits=[("vd", n_st), rc_g[sl]])
            n_xn = em.op("scalar", lambda e, sl=sl: e.activation(
                xntm[sl][:], xc[sl][:], Act.Identity, scale=rc[sl][:]),
                inc="ad", waits=[("vd", n_rc), xntm_g[sl]])
            rc_g[sl] = [("ad", n_xn)]
            last_tmm = 0
            n_evx = 0
            for half in range(2):
                tbis = []
                for cc in range(half * 8, half * 8 + 8):
                    bi = banks.take()
                    em.wait("tensor", "ad", n_xn)
                    em.wait("tensor", "vd", n_idb)
                    last_tmm = em.op("tensor", lambda e, bi=bi, cc=cc, sl=sl: e.transpose(
                        pbb[bi][:], xntm[sl][:, cc * 128:(cc + 1) * 128],
                        ident_bf[:]), inc="mm")
                    tbis.append((cc, bi))
                n_ge = em.op("gpsimd", lambda e: e.nop(), inc="gc",
                             waits=[("mm", last_tmm)])
                for cc, bi in tbis:
                    n_evx = em.op("vector", lambda e, bi=bi, cc=cc, t=t: e.tensor_copy(
                        xn_fm[:, cc * S + t * 128: cc * S + t * 128 + 128],
                        pbb[bi][:]), inc="vd", waits=[("gc", n_ge)])
                    banks.done(bi, "vd", n_evx)
            xn_ev_vd[t] = n_evx
            xn_ev[t] = 0
            xntm_g[sl] = [("mm", last_tmm)]
            n_q1 = em.op("vector", lambda e, sl=sl, t=t: e.tensor_scalar(
                xA[sl][:], xc[sl][:], st["s3"][:, t:t + 1], MAGIC,
                Alu.mult, Alu.add),
                inc="vd", waits=[("vd", n_st), xA_g[sl]])
            bsl = t % 2
            n_q2 = em.op("vector", lambda e, sl=sl, bsl=bsl: e.tensor_scalar(
                qbf[bsl][:], xA[sl][:], -MAGIC, None, Alu.add),
                inc="vd", waits=[("vd", n_q1), qbf_g[bsl]])
            xA_g[sl] = [("vd", n_q2)]
            n_sq_ = em.dma("gpsimd", d_xqsc[t * 128:(t + 1) * 128, :],
                           qbf[bsl][:], f"sxq{bsl}", waits=[("vd", n_q2)])
            qbf_g[bsl] = [(f"sxq{bsl}", n_sq_)]
            xc_g[sl] = [("vd", em.cur("vd")), ("ad", em.cur("ad"))]

        if pr % 2 == 1:
            emit_slab_kv(pr // 2)

    kvb = [banks.take() for _ in range(4)]
    ksb = [banks.take() for _ in range(4)]
    for b in kvb + ksb:
        banks.pin(b)
    for t in range(NT):
        em.wait("tensor", "vd", sk_ready[t])
        em.wait("tensor", "ad", v_ready[t])
        for gk in range(GK):
            em.op("tensor", lambda e, gk=gk, t=t: e.matmul(
                pslice(kvb[gk], 128),
                sk_tm[:, t * KD + gk * 128: t * KD + gk * 128 + 128],
                v_tm[:, t * KD + gk * 128: t * KD + gk * 128 + 128],
                start=(t == 0), stop=(t == NT - 1), skip_group_check=True))
            em.op("tensor", lambda e, gk=gk, t=t: e.matmul(
                pslice(ksb[gk], 1),
                sk_tm[:, t * KD + gk * 128: t * KD + gk * 128 + 128],
                ones_cb[:], start=(t == 0), stop=(t == NT - 1),
                skip_group_check=True),
                inc=("mm" if (t == NT - 1 and gk == GK - 1) else None))
    n_kvmm = em.cur("mm")
    n_kvec = echo_mm(n_kvmm)
    n_kv = [0] * GK
    for gk in range(GK):
        em.op("vector", lambda e, gk=gk: e.tensor_copy(
            kv_sb[:, gk * 132: gk * 132 + 128], pslice(kvb[gk], 128)),
            waits=[("ad", n_kvec)])
        n_kv[gk] = em.op("vector", lambda e, gk=gk: e.tensor_copy(
            kv_sb[:, gk * 132 + 128: gk * 132 + 129], pslice(ksb[gk], 1)),
            inc="vd", waits=[("ad", n_kvec)])
    for gk in range(GK):
        banks.unpin(kvb[gk], "vd", n_kv[gk])
        banks.unpin(ksb[gk], "vd", n_kv[gk])
    n_kemb = 0
    for h in range(NH):
        gk = h // 4
        n_kemb = em.op("vector", lambda e, h=h, gk=gk: e.tensor_copy(
            kemb[:, h * 16 + h: h * 16 + h + 1],
            kv_sb[:, gk * 132 + 128: gk * 132 + 129]), inc="vd",
            waits=[("vd", max(n_kv[gk], n_kem0))])

    n_A_ad = em.cur("ad")
    n_A_vd = em.cur("vd")
    n_A_st = [("sxq0", em.cur("sxq0")), ("sxq1", em.cur("sxq1"))]
    A_dead = [("ad", n_A_ad), ("vd", n_A_vd)] + n_A_st

    n_sel = em.dma("gpsimd", sel_bf[:], dsel, "lsel", waits=[A_dead])

    qstr_g = [A_dead, A_dead]
    hbuf_g[0] = A_dead
    hbuf_g[1] = A_dead
    qtmp_g[0] = A_dead
    qbfj_g[0] = A_dead
    for i in range(NH):
        sl = i % 2
        n_ld = em.dma("gpsimd", qstr[sl][:], dwq[i * 128:(i + 1) * 128, :],
                      f"lq{sl}", waits=[qstr_g[sl]])
        for g in range(TG):
            bi = banks.take()
            em.wait("tensor", f"lq{sl}", n_ld)
            em.wait("tensor", "vd", n_A_vd)
            for cc in range(C):
                em.op("tensor", lambda e, bi=bi, sl=sl, cc=cc, g=g: e.matmul(
                    pb[bi][:], qstr[sl][:, cc * 128:(cc + 1) * 128],
                    xn_fm[:, cc * S + g * 512: cc * S + g * 512 + 512],
                    start=(cc == 0), stop=(cc == C - 1)),
                    inc=("mm" if cc == C - 1 else None))
            nmm = em.cur("mm")
            p2 = (i * TG + g) % 2
            n_e = em.op("scalar", lambda e, bi=bi, i=i, p2=p2: e.activation(
                e_bf[p2][:], pb[bi][:], Act.Exp, bias=bqc[i][:]),
                inc="ad", waits=[("mm", nmm), ("vd", n_bqc), e_g[p2]])
            n_r = em.op("vector", lambda e, bi=bi, i=i, p2=p2: e.tensor_scalar(
                rp_bf[p2][:], pb[bi][:], bq1_t[:, i:i + 1], 1.0, Alu.add, Alu.max),
                inc="vd", waits=[("ad", n_e), ("vd", n_bq1), rp_g[p2]])
            banks.done(bi, "vd", n_r)
            n_m = em.op("vector", lambda e, i=i, g=g, p2=p2: e.tensor_tensor(
                sq_fm[:, i * S + g * 512: i * S + g * 512 + 512],
                e_bf[p2][:], rp_bf[p2][:], Alu.min), inc="vd",
                waits=[("vd", n_r)])
            e_g[p2] = [("vd", n_m)]
            rp_g[p2] = [("vd", n_m)]
        if i > 0:
            emit_wq_back(i - 1)
        emit_wq(i)
        qstr_g[sl] = [("mm", em.cur("mm"))]
    emit_wq_back(NH - 1)
    n_qproj_mm = em.cur("mm")
    n_sq_done = em.cur("vd")
    wq_dead = [("vd", em.cur("vd")), ("ad", em.cur("ad")),
               ("mm", em.cur("mm"))]

    xq_ld_n = [0] * C

    def load_xq(cc, eng, extra):
        xq_ld_n[cc] = em.dma(
            eng, xq_fm[:, cc * S: cc * S + S],
            d_xqsc[:, cc * 128:(cc + 1) * 128], "lxq",
            waits=[A_dead] + list(extra), transpose=True)

    for cc in range(4, 12):
        load_xq(cc, "sync", [])
    for cc in range(12, 16):
        load_xq(cc, "sync",
                wq_dead + [("swq0", em.cur("swq0")), ("swq1", em.cur("swq1"))])

    rb_g = [A_dead, A_dead]
    attn_g = [0] * TG
    rec_guard = None
    for g in range(TG):
        if g == 1:
            load_xq(0, "sync", [("mm", n_qproj_mm)])
            load_xq(1, "sync", [("mm", n_qproj_mm)])
        bi_n = banks.take()
        em.wait("tensor", "vd", n_sq_done)
        em.wait("tensor", "vd", n_kemb)
        banks.pin(bi_n)
        for h in range(NH):
            em.op("tensor", lambda e, bi=bi_n, h=h, g=g: e.matmul(
                pb[bi][0:16, 0:512], kemb[:, h * 16:(h + 1) * 16],
                sq_fm[:, h * S + g * 512: h * S + g * 512 + 512],
                start=(h == 0), stop=(h == NH - 1)),
                inc=("mm" if h == NH - 1 else None))
        n_nmm = em.cur("mm")
        n_nec = echo_mm(n_nmm)
        n_re = em.op("vector", lambda e, bi=bi_n: e.tensor_scalar(
            pb[bi][32:48, 0:512], pb[bi][0:16, 0:512], 1e-6, None, Alu.add),
            inc="vd", waits=[("ad", n_nec)])
        n_rf = em.op("vector", lambda e, bi=bi_n: e.reciprocal(
            pb[bi][0:16, 0:512], pb[bi][32:48, 0:512]), inc="vd",
            waits=[("vd", n_re)])
        n_rb16 = em.op("vector", lambda e, bi=bi_n: e.tensor_copy(
            rec_bf[:], pb[bi][0:16, 0:512]), inc="vd",
            waits=[("vd", n_rf), rec_guard])
        banks.unpin(bi_n, "vd", n_rb16)
        for h in range(NH):
            p2 = h % 2
            bi_r = banks.take()
            em.wait("tensor", "vd", n_rb16)
            em.wait("tensor", "lsel", n_sel)
            n_rmm = em.op("tensor", lambda e, bi=bi_r, h=h: e.matmul(
                pb[bi][:], sel_bf[:, h * 128:(h + 1) * 128], rec_bf[:],
                start=True, stop=True), inc="mm")
            bi_x = banks.take()
            gk = h // 4
            em.wait("tensor", "vd", n_kv[gk])
            n_xmm = em.op("tensor", lambda e, bi=bi_x, h=h, g=g, gk=gk: e.matmul(
                pb[bi][:], kv_sb[:, gk * 132: gk * 132 + 128],
                sq_fm[:, h * S + g * 512: h * S + g * 512 + 512],
                start=True, stop=True), inc="mm")
            n_rbe = em.op("scalar", lambda e, bi=bi_r, p2=p2: e.activation(
                rb_sb[p2][:], pb[bi][:], Act.Identity), inc="ad",
                waits=[("mm", n_xmm), rb_g[p2]])
            banks.done(bi_r, "ad", n_rbe)
            n_div = em.op("vector", lambda e, bi=bi_x, h=h, g=g, p2=p2: e.tensor_tensor(
                attn_fm[:, h * S + g * 512: h * S + g * 512 + 512],
                pb[bi][:], rb_sb[p2][:], Alu.mult), inc="vd",
                waits=[("ad", n_rbe)])
            banks.done(bi_x, "vd", n_div)
            rb_g[p2] = [("vd", n_div)]
        rec_guard = [("ad", echo_mm(em.cur("mm")))]
        attn_g[g] = em.cur("vd")
    n_pd_mm = em.cur("mm")
    n_attn_vd = em.cur("vd")

    load_xq(2, "sync", [("mm", n_pd_mm)])
    load_xq(3, "sync", [("vd", n_attn_vd)])
    xq_all = em.cur("lxq")

    bi_t = banks.take()
    n_xst = em.op("tensor", lambda e, bi=bi_t: e.transpose(
        pb[bi][0:16, 0:128], st["xs"][:, 0:16], ident[:]), inc="mm",
        waits=[("vd", n_stats_last)])
    n_xsr = em.op("scalar", lambda e, bi=bi_t: e.activation(
        xsr_sb[:], pb[bi][0:16, 0:128], Act.Identity), inc="ad",
        waits=[("mm", max(n_xst, n_pd_mm))])
    banks.done(bi_t, "ad", n_xsr)
    n_xsst = em.dma("gpsimd", d_xsr.rearrange("a (b c) -> (a b) c", b=16),
                    xsr_sb[:], "sxr", waits=[("ad", n_xsr)])
    n_xrow = em.dma("gpsimd", xs_row[:], d_xsr, "lxr",
                    waits=[("sxr", n_xsst), ("mm", n_pd_mm)])
    xsb_ev = 0
    for g in range(TG):
        bi = banks.take()
        em.wait("tensor", "lxr", n_xrow)
        nmm = em.op("tensor", lambda e, bi=bi, g=g: e.matmul(
            pb[bi][:], ones_rf[:], xs_row[:, g * 512:(g + 1) * 512],
            start=True, stop=True), inc="mm")
        xsb_ev = em.op("scalar", lambda e, bi=bi, g=g: e.activation(
            xs_b[:, g * 512:(g + 1) * 512], pb[bi][:], Act.Identity),
            inc="ad", waits=[("mm", nmm)])
        banks.done(bi, "ad", xsb_ev)

    wostr_g = [None, None]
    fstr_g = [None, None]
    xt_g = [None, None]
    ost_g = [None, None]
    ystg_g = [None, None]
    ya_g = [None, None]
    yb_g = [None, None]
    ostage_g = [None, None]
    n_swq = [("swq0", em.cur("swq0")), ("swq1", em.cur("swq1"))]
    for i in range(C):
        sl = i % 2
        n_wol = em.dma("gpsimd", wostr[sl][:], dwo[i * 128:(i + 1) * 128, :],
                       f"lo{sl}", waits=[("mm", n_pd_mm), wostr_g[sl]])
        n_fl = 0
        for cc in range(C):
            n_fl = em.dma("gpsimd", fstr[sl][:, cc * 128:(cc + 1) * 128],
                          d_wq2[(cc * 16 + i) * 128:(cc * 16 + i) * 128 + 128, :],
                          f"lf{sl}",
                          waits=[("mm", n_pd_mm)] + n_swq + [fstr_g[sl]])
        for g in range(TG):
            xsl = (i * TG + g) % 2
            n_xl = em.dma("scalar", xtile[xsl][:],
                          dxp[(i * TG + g) * 128:(i * TG + g) * 128 + 128, :],
                          f"lxt{xsl}", waits=[("mm", n_pd_mm), xt_g[xsl]])
            biA = banks.take()
            em.wait("tensor", f"lo{sl}", n_wol)
            em.wait("tensor", "vd", attn_g[g])
            for cc in range(C):
                em.op("tensor", lambda e, bi=biA, sl=sl, cc=cc, g=g: e.matmul(
                    pb[bi][:], wostr[sl][:, cc * 128:(cc + 1) * 128],
                    attn_fm[:, cc * S + g * 512: cc * S + g * 512 + 512],
                    start=(cc == 0), stop=(cc == C - 1)),
                    inc=("mm" if cc == C - 1 else None))
            nmmA = em.cur("mm")
            biB = banks.take()
            em.wait("tensor", f"lf{sl}", n_fl)
            em.wait("tensor", "lxq", xq_all)
            for cc in range(C):
                em.op("tensor", lambda e, bi=biB, sl=sl, cc=cc, g=g: e.matmul(
                    pb[bi][:], fstr[sl][:, cc * 128:(cc + 1) * 128],
                    xq_fm[:, cc * S + g * 512: cc * S + g * 512 + 512],
                    start=(cc == 0), stop=(cc == C - 1)),
                    inc=("mm" if cc == C - 1 else None))
            nmmB = em.cur("mm")
            osl = (i * TG + g) % 2
            n_evA = em.op("scalar", lambda e, bi=biA, osl=osl, i=i: e.activation(
                ostg0[osl][:], pb[bi][:], Act.Identity, bias=fbc[i][:]),
                inc="ad", waits=[("mm", nmmA), ost_g[osl]])
            banks.done(biA, "ad", n_evA)
            n_evB = em.op("scalar", lambda e, bi=biB, osl=osl, i=i: e.activation(
                ystg0[osl][:], pb[bi][:], Act.Identity, scale=wscc[i][:]),
                inc="ad", waits=[("mm", nmmB), ystg_g[osl]])
            banks.done(biB, "ad", n_evB)
            n_y1 = em.op("vector", lambda e, osl=osl, g=g: e.tensor_tensor(
                ysa[osl][:], ystg0[osl][:], xs_b[:, g * 512:(g + 1) * 512],
                Alu.mult), inc="vd",
                waits=[("ad", n_evB), ("ad", xsb_ev), ya_g[osl]])
            ystg_g[osl] = [("vd", n_y1)]
            n_y2 = em.op("vector", lambda e, osl=osl: e.tensor_tensor(
                ysb[osl][:], ysa[osl][:], ostg0[osl][:], Alu.add), inc="vd",
                waits=[("ad", n_evA), yb_g[osl]])
            ya_g[osl] = [("vd", n_y2)]
            ost_g[osl] = [("vd", n_y2)]
            n_y3 = em.op("vector", lambda e, osl=osl, xsl=xsl: e.tensor_tensor(
                ostage[osl][:], ysb[osl][:], xtile[xsl][:], Alu.add), inc="vd",
                waits=[(f"lxt{xsl}", n_xl), ostage_g[osl]])
            yb_g[osl] = [("vd", n_y3)]
            xt_g[xsl] = [("vd", n_y3)]
            n_os = em.dma("sync",
                          dout[i * 128:(i + 1) * 128, g * 512:(g + 1) * 512],
                          ostage[osl][:], f"sob{osl}", waits=[("vd", n_y3)])
            ostage_g[osl] = [(f"sob{osl}", n_os)]
        wostr_g[sl] = [("mm", em.cur("mm"))]
        fstr_g[sl] = [("mm", em.cur("mm"))]

    return _finish(nc, em, sem_names)


def _finish(nc, em, sem_names):
    for sname in ("sob0", "sob1"):
        em.wait("sync", sname, em.cur(sname))
        em.wait("gpsimd", sname, em.cur(sname))
    for sname in ("sxq0", "sxq1", "swq0", "swq1", "sxr"):
        em.wait("gpsimd", sname, em.cur(sname))
    em.wait("vector", "mm", em.cur("mm"))
    em.wait("scalar", "mm", em.cur("mm"))

    for s in sem_names:
        em.sems[s] = nc.alloc_semaphore(s)
    with nc.Block() as block:

        @block.sync
        def _(e):
            for f in em.scripts["sync"]:
                f(e)

        @block.tensor
        def _(e):
            for f in em.scripts["tensor"]:
                f(e)

        @block.vector
        def _(e):
            for f in em.scripts["vector"]:
                f(e)

        @block.scalar
        def _(e):
            for f in em.scripts["scalar"]:
                f(e)

        @block.gpsimd
        def _(e):
            for f in em.scripts["gpsimd"]:
                f(e)

    return nc


def make_inputs(b, x, Wq, bq, Wk, bk, Wv, bv, Wo, Wf, fb):
    f32 = np.float32
    xb = np.ascontiguousarray(x[b], dtype=f32)
    x_pk = np.ascontiguousarray(
        xb.T.reshape(C, 128, TG, 512).transpose(0, 2, 1, 3)).reshape(C * TG * 128, 512)
    wq_pk = np.ascontiguousarray(
        Wq.reshape(NH, 128, C, 128).transpose(0, 3, 2, 1)).reshape(NH * 128, D)
    wo_pk = np.ascontiguousarray(
        Wo.reshape(C, 128, C, 128).transpose(0, 3, 2, 1)).reshape(C * 128, D)
    wk_pk = np.ascontiguousarray(
        Wk.T.reshape(C, 128, KD).transpose(1, 0, 2)).reshape(128, C * KD)
    wv_pk = np.ascontiguousarray(
        Wv.T.reshape(C, 128, KD).transpose(1, 0, 2)).reshape(128, C * KD)
    sel = np.zeros((16, D), f32)
    for h in range(16):
        sel[h, h * 128:(h + 1) * 128] = 1.0
    return {
        "x_nat": xb,
        "x_pk": x_pk,
        "wq_pk": wq_pk,
        "wo_pk": wo_pk,
        "wk_pk": wk_pk,
        "wv_pk": wv_pk,
        "wn_nat": np.ascontiguousarray(Wf, dtype=f32),
        "bq_t": np.ascontiguousarray(bq.reshape(16, 128).T),
        "fb_t": np.ascontiguousarray(fb.reshape(16, 128).T),
        "bk_r": np.ascontiguousarray(bk[None, :]),
        "bv_r": np.ascontiguousarray(bv[None, :]),
        "sel_h": sel,
        "ident": np.eye(128, dtype=f32),
    }


def kernel(L0_attn_hidden_states, L0_attn_Wq, L0_attn_bq, L0_attn_Wk,
           L0_attn_bk, L0_attn_Wv, L0_attn_bv, L0_attn_Wo, L0_ffn_weight,
           L0_ffn_bias):
    x = np.asarray(L0_attn_hidden_states, dtype=np.float32)
    Wq = np.asarray(L0_attn_Wq, dtype=np.float32)
    bq = np.asarray(L0_attn_bq, dtype=np.float32)
    Wk = np.asarray(L0_attn_Wk, dtype=np.float32)
    bk = np.asarray(L0_attn_bk, dtype=np.float32)
    Wv = np.asarray(L0_attn_Wv, dtype=np.float32)
    bv = np.asarray(L0_attn_bv, dtype=np.float32)
    Wo = np.asarray(L0_attn_Wo, dtype=np.float32)
    Wf = np.asarray(L0_ffn_weight, dtype=np.float32)
    fb = np.asarray(L0_ffn_bias, dtype=np.float32)
    nc = build_nc()
    in_maps = [make_inputs(b, x, Wq, bq, Wk, bk, Wv, bv, Wo, Wf, fb)
               for b in range(8)]
    res = run_bass_kernel_spmd(nc, in_maps, core_ids=list(range(8)))
    out = np.stack([np.ascontiguousarray(r["out"].T) for r in res.results])
    return out
